# version 35 (final)
# baseline (speedup 1.0000x reference)
import numpy as np
import concourse.bass as bass
import concourse.mybir as mybir
from concourse.bass_utils import run_bass_kernel_spmd

F32 = mybir.dt.float32
BF16 = mybir.dt.bfloat16
F16 = mybir.dt.float16
F8 = mybir.dt.float8e4
U8 = mybir.dt.uint8
AF = mybir.ActivationFunctionType
ALU = mybir.AluOpType
AX = mybir.AxisListType

D = 1024
NDC = D // 128
SCALE = float(D ** (-0.25))
LAMBDA_INIT = 0.8
EPS = 1e-5
OUT_MUL = 1.0 - LAMBDA_INIT
STRIPE = 2
DEBUG_STOP = None

B = S = NCORES = SEQ_PER_CORE = T = NT = NTS = NTB = None


def _set_dims(b, s, ncores):
    global B, S, NCORES, SEQ_PER_CORE, T, NT, NTS, NTB
    B, S, NCORES = b, s, ncores
    SEQ_PER_CORE = B // NCORES
    T = SEQ_PER_CORE * S
    NT = T // 128
    NTS = S // 128
    NTB = S // 512
    assert S % 512 == 0 and NTS % STRIPE == 0


_set_dims(16, 2048, 8)


class Sem:
    def __init__(self, handle, name):
        self.h = handle
        self.name = name
        self.n = 0

    def bump(self, k=1):
        self.n += k
        return self.n

    @property
    def tok(self):
        return (self, self.n)


class ES:
    def __init__(self, name):
        self.name = name
        self.ops = []
        self._waited = {}

    def wait(self, sem, val=None):
        if isinstance(sem, tuple):
            sem, val = sem
        if val is None:
            val = sem.n
        if val <= 0:
            return
        if self._waited.get(sem.name, -1) >= val:
            return
        self._waited[sem.name] = val
        self.ops.append(("w", sem, val))

    def op(self, meth, *args, inc=None, inc_k=1, **kw):
        if inc is not None:
            inc.bump(inc_k)
            self.ops.append(("o", meth, args, kw, inc, inc_k))
            return inc.n
        self.ops.append(("o", meth, args, kw, None, 0))
        return None

    def replay(self, eng):
        for item in self.ops:
            if item[0] == "w":
                eng.wait_ge(item[1].h, item[2])
            else:
                _, meth, args, kw, inc, k = item
                instr = getattr(eng, meth)(*args, **kw)
                if inc is not None:
                    instr.then_inc(inc.h, k)


class Arena:
    def __init__(self, nc, total_bytes):
        self.nc = nc
        self.guard = nc.sbuf_tensor("arena", [128, total_bytes], U8)
        self.h = self.guard.__enter__()
        self.base = nc.lookup_mloc(self.h).addr
        self.per_off = 0
        self.ph_off = None
        self.total = total_bytes
        self.uid = 0

    def _alloc(self, shape, dtype, off):
        per_part = int(np.prod(shape[1:])) * mybir.dt.size(dtype)
        off_al = (off + 31) // 32 * 32
        name = f"ar{self.uid}"
        self.uid += 1
        hnd = self.nc.virtual_sbuf_tensor(name, shape, dtype)
        self.nc._tensor(hnd.name, list(shape), dtype, type="SB")
        mloc = self.nc.lookup_mloc(hnd)
        mloc.allocated = True
        mloc.addr = self.base + off_al
        return hnd, off_al + per_part

    def persist(self, shape, dtype):
        h, end = self._alloc(shape, dtype, self.per_off)
        self.per_off = end
        assert self.per_off <= self.total, f"persist overflow {self.per_off}"
        return h

    def phase_begin(self):
        self.ph_off = self.per_off

    def mark(self):
        return self.ph_off

    def at(self, off, shape, dtype):
        h, end = self._alloc(shape, dtype, off)
        assert end <= self.total
        return h

    def phase(self, shape, dtype):
        h, end = self._alloc(shape, dtype, self.ph_off)
        self.ph_off = end
        assert self.ph_off <= self.total, \
            f"phase overflow {self.ph_off} > {self.total}"
        return h


def build_nc():
    nc = bass.Bass("TRN2", target_bir_lowering=False)

    x_in = nc.dram_tensor("x", [T, D], F32, kind="ExternalInput").ap()
    wq_in = nc.dram_tensor("wq", [D, 2 * D], F32, kind="ExternalInput").ap()
    wk_in = nc.dram_tensor("wk", [D, 2 * D], F32, kind="ExternalInput").ap()
    wv_in = nc.dram_tensor("wv", [D, D], F32, kind="ExternalInput").ap()
    wfT_in = nc.dram_tensor("wfT", [D, D], F32, kind="ExternalInput").ap()
    bias_in = nc.dram_tensor("bias", [1, D], F32, kind="ExternalInput").ap()
    lq1_in = nc.dram_tensor("lq1", [128, 8], F32, kind="ExternalInput").ap()
    lk1_in = nc.dram_tensor("lk1", [128, 8], F32, kind="ExternalInput").ap()
    lq2_in = nc.dram_tensor("lq2", [128, 8], F32, kind="ExternalInput").ap()
    lk2_in = nc.dram_tensor("lk2", [128, 8], F32, kind="ExternalInput").ap()
    out_ext = nc.dram_tensor("out", [T, D], F32, kind="ExternalOutput").ap()

    h_s = nc.dram_tensor("h_s", [T, D], F32, kind="Internal").ap()
    h2_s = nc.dram_tensor("h2_s", [T, D], F32, kind="Internal").ap()
    SWc = 128 * STRIPE
    NSP = S // SWc
    y2h_s = nc.dram_tensor("y2h_s", [SEQ_PER_CORE, NSP, 128, NDC * SWc], BF16, kind="Internal").ap()
    y2l_s = nc.dram_tensor("y2l_s", [SEQ_PER_CORE, NSP, 128, NDC * SWc], BF16, kind="Internal").ap()
    vh_s = nc.dram_tensor("vh_s", [NTS, 128, D], BF16, kind="Internal").ap()
    vl_s = nc.dram_tensor("vl_s", [NTS, 128, D], BF16, kind="Internal").ap()
    wqh_s = nc.dram_tensor("wqh_s", [NDC, 128, 2 * D], BF16, kind="Internal").ap()
    wql_s = nc.dram_tensor("wql_s", [NDC, 128, 2 * D], BF16, kind="Internal").ap()
    wkh_s = nc.dram_tensor("wkh_s", [16, 128, NDC * 128], BF16, kind="Internal").ap()
    wkl_s = nc.dram_tensor("wkl_s", [16, 128, NDC * 128], BF16, kind="Internal").ap()
    wvh_s = nc.dram_tensor("wvh_s", [NDC, 128, D], BF16, kind="Internal").ap()
    wvl_s = nc.dram_tensor("wvl_s", [NDC, 128, D], BF16, kind="Internal").ap()

    import contextlib
    stack = contextlib.ExitStack()
    E = stack.enter_context

    ps = E(nc.psum_tensor("ps", [128, 4096], F32))

    def bank(b, n=1):
        return ps[:, b * 512:(b + n) * 512]

    sem_names = ["pe", "ve", "se",
                 "b_sy", "b_pe", "b_ve", "b_se", "b_gp",
                 "Lident", "Lli", "Lw32", "Lx0", "Lx1", "Ly2sp", "Lwv0", "Lwv1", "Lwv2", "Lwv3",
                 "Lwk0", "Lwk1", "Lwk2", "Lwk3", "Ly2t", "Lwq0", "Lwq1", "Lwq2", "Lwq3", "Lwq4", "Lwq5", "Lv0", "Lv1", "Lv2", "Lht", "Lw32b", "Lw32c", "Lw32d", "Lx2", "Lx3",
                 "Sw", "Swb", "Swc", "Swd", "Sh0", "Sh1", "Sy0", "Sy1", "Sv0", "Sv1", "Sh2"]
    sems = {nm: Sem(E(nc.semaphore("sem_" + nm)), nm) for nm in sem_names}
    s_pe, s_ve, s_se = sems["pe"], sems["ve"], sems["se"]
    store_sems = [sems[nm] for nm in
                  ["Sw", "Swb", "Swc", "Swd", "Sh0", "Sh1", "Sy0", "Sy1", "Sv0",
                   "Sv1", "Sh2"]]

    SY, PE, VE, SE, GP = ES("sync"), ES("tensor"), ES("vector"), ES("scalar"), ES("gpsimd")
    ALL = {"sync": SY, "tensor": PE, "vector": VE, "scalar": SE, "gpsimd": GP}
    bar_sem = {"sync": sems["b_sy"], "tensor": sems["b_pe"], "vector": sems["b_ve"],
               "scalar": sems["b_se"], "gpsimd": sems["b_gp"]}
    bar_count = [0]

    def barrier(skip_stores=()):
        bar_count[0] += 1
        k = bar_count[0]
        for ssem in store_sems:
            if ssem.name not in skip_stores:
                GP.wait(ssem, ssem.n)
        for nm, es in ALL.items():
            es.op("drain", inc=bar_sem[nm])
        for nm, es in ALL.items():
            for nm2, sem2 in bar_sem.items():
                if nm2 != nm:
                    es.wait(sem2, k)

    AR_BYTES = 212832
    ar = Arena(nc, AR_BYTES)

    ident = ar.persist([128, 128], F32)
    ones_col = ar.persist([128, 1], F32)
    ones_row = ar.persist([1, 128], F32)
    lam_vec = ar.persist([128, 1], F32)
    stat = ar.persist([128, 64], F32)
    sml = ar.persist([1, 16], F32)
    eps_vec = ar.persist([128, 1], F32)
    bias_p0 = ar.persist([1, D], F32)

    attn_base = (ar.per_off + 31) // 32 * 32
    k_sb = {}
    _o = attn_base
    for m in (1, 2):
        for p in ("h", "l"):
            k_sb[(m, p)] = ar.at(_o, [128, NDC * S], BF16)
            _o += NDC * S * 2
    ATTN_PH = _o

    def attn_phase_begin():
        ar.ph_off = ATTN_PH

    ident_h = nc.inline_tensor(np.eye(128, dtype=np.float32), name="identity_c")

    (C_BN, C_MEAN, C_VAR, C_SQ, C_RSTD, C_NEGM,
     C_MB, C_MA, C_MBB, C_NMS1, C_NMS2, C_SUM1A, C_SUM1B, C_SUM2,
     C_R2, C_C, C_NINV1A, C_NINV1B, C_SSQ, C_RMS, C_FS) = \
        (0, 12, 13, 14, 15, 16, 17, 21, 22, 23, 24, 25, 26, 27, 28, 29, 30, 31, 32, 33, 34)
    (C_BN2, C_MEAN2, C_VAR2, C_SQ2, C_RSTD2, C_NEGM2) = (36, 48, 49, 50, 51, 52)

    def st(c, n=1):
        return stat[:, c:c + n]

    ar.phase_begin()
    lam_bufs = [ar.phase([128, 8], F32) for _ in range(4)]

    s_li, s_w32, s_sw = sems["Lli"], sems["Lw32"], sems["Sw"]
    SY.op("dma_start", out=ident[:], in_=ident_h.ap(), inc=sems["Lident"], inc_k=16)
    for i, tin in enumerate([lq1_in, lk1_in, lq2_in, lk2_in]):
        SY.op("dma_start", out=lam_bufs[i][:], in_=tin, inc=s_li, inc_k=16)
    lam_ld = s_li.tok
    VE.op("memset", ones_col[:], 1.0)
    VE.op("memset", ones_row[:], 1.0)
    VE.op("memset", eps_vec[:], float(EPS))
    VE.op("drain", inc=s_ve)
    ones_done = s_ve.n
    VE.wait(lam_ld)
    VE.op("tensor_mul", lam_bufs[0][:], lam_bufs[0][:], lam_bufs[1][:])
    VE.op("tensor_mul", lam_bufs[2][:], lam_bufs[2][:], lam_bufs[3][:])
    VE.op("drain")
    VE.op("reduce_sum", lam_bufs[1][:, 0:1], lam_bufs[0][:], axis=AX.X)
    VE.op("reduce_sum", lam_bufs[3][:, 0:1], lam_bufs[2][:], axis=AX.X)
    VE.op("drain", inc=s_ve)
    lam_red = s_ve.n
    PE.wait(s_ve, lam_red)
    PE.op("matmul", ps[0:1, 0:1], ones_col[:], lam_bufs[1][:, 0:1], start=True, stop=True)
    PE.op("matmul", ps[0:1, 512:513], ones_col[:], lam_bufs[3][:, 0:1], start=True,
          stop=True, inc=s_pe)
    lam_mm = s_pe.n
    SE.wait(s_pe, lam_mm)
    SE.op("activation", sml[:, 0:1], ps[0:1, 0:1], AF.Exp)
    SE.op("activation", sml[:, 1:2], ps[0:1, 512:513], AF.Exp)
    SE.op("drain", inc=s_se)
    lam_exp = s_se.n
    VE.wait(s_se, lam_exp)
    VE.op("tensor_sub", sml[:, 2:3], sml[:, 0:1], sml[:, 1:2])
    VE.op("drain")
    VE.op("tensor_scalar_add", sml[:, 2:3], sml[:, 2:3], float(LAMBDA_INIT))
    VE.op("drain", inc=s_ve)
    lam_f = s_ve.n
    PE.wait(s_ve, lam_f)
    PE.wait(s_ve, ones_done)
    PE.op("matmul", ps[:, 1024:1025], ones_row[:], sml[:, 2:3], start=True, stop=True,
          inc=s_pe)
    lam_b = s_pe.n
    SE.wait(s_pe, lam_b)
    SE.op("copy", lam_vec[:], ps[:, 1024:1025], inc=s_se)

    split_done = [None]
    def _split_units():
        units = []
        for dc in range(NDC):
            r0, r1 = dc * 128, (dc + 1) * 128
            for q4 in range(2):
                units.append((wv_in[r0:r1, q4 * 512:(q4 + 1) * 512],
                              [(wvh_s[dc, :, q4 * 512:(q4 + 1) * 512],
                                wvl_s[dc, :, q4 * 512:(q4 + 1) * 512], 0, 512)]))
        for dc in range(NDC):
            r0, r1 = dc * 128, (dc + 1) * 128
            for q4 in range(4):
                units.append((wk_in[r0:r1, q4 * 512:(q4 + 1) * 512],
                              [(wkh_s[et, :, dc * 128:(dc + 1) * 128],
                                wkl_s[et, :, dc * 128:(dc + 1) * 128],
                                (et - q4 * 4) * 128, (et - q4 * 4 + 1) * 128)
                               for et in range(q4 * 4, q4 * 4 + 4)]))
        for dc in range(NDC):
            r0, r1 = dc * 128, (dc + 1) * 128
            for q4 in range(4):
                units.append((wq_in[r0:r1, q4 * 512:(q4 + 1) * 512],
                              [(wqh_s[dc, :, q4 * 512:(q4 + 1) * 512],
                                wql_s[dc, :, q4 * 512:(q4 + 1) * 512], 0, 512)]))
        return units

    def emit_ffn(src_hbm, dst_hbm, ln2=False, splits=None, skip=None):
        ar.phase_begin()
        w8 = ar.phase([128, NDC * D], F8)
        xt = [ar.phase([128, D], F32) for _ in range(4)]
        y1f = [ar.phase([128, D], F32) for _ in range(2)]
        y8 = [ar.phase([128, D], F8) for _ in range(2)]
        f16t = [ar.phase([128, D], F16) for _ in range(2)]
        ht = [ar.phase([128, D], F32) for _ in range(2)]
        s_x = [sems["Lx0"], sems["Lx1"], sems["Lx2"], sems["Lx3"]]
        s_sh = [sems["Sh0"], sems["Sh1"]]
        if ln2:
            y2f_1 = ar.phase([128, D], F32)
            y2f = [y2f_1, y2f_1]
            y2sph = ar.phase([128, NDC * 512], BF16)
            y2spl = ar.phase([128, NDC * 512], BF16)
            s_sy = [sems["Sy0"], sems["Sy1"]]
        if splits:
            w32s = [ar.phase([128, 512], F32) for _ in range(4)]
            whis = [ar.phase([128, 512], BF16) for _ in range(4)]
            wlos = [ar.phase([128, 512], BF16) for _ in range(4)]
            spl_state = {"i": 0}
            spl_sems = [sems["Lw32"], sems["Lw32b"], sems["Lw32c"], sems["Lw32d"]]
            spl_st = [sems["Sw"], sems["Swb"], sems["Swc"], sems["Swd"]]
            wr = w32s
        else:
            wr = [ar.phase([128, 512], F32) for _ in range(4)]

        def issue_split_load(j):
            if j >= len(splits) or ("ld", j) in spl_state:
                return
            sl = j % 4
            src_ap_, _ = splits[j]
            if j >= 4:
                SY.wait(spl_state[("st", j - 4)])
            else:
                SY.wait(spl_sems[sl], spl_sems[sl].n)
                SY.wait(s_se, s_se.n)
            SY.op("dma_start", out=w32s[sl][:], in_=src_ap_, inc=spl_sems[sl],
                  inc_k=16)
            spl_state[("ld", j)] = spl_sems[sl].tok

        def emit_split_unit():
            j = spl_state["i"]
            if j >= len(splits):
                return
            spl_state["i"] += 1
            sl = j % 4
            _, writes = splits[j]
            issue_split_load(j)
            SE.wait(spl_state[("ld", j)])
            if j >= 4:
                SE.wait(spl_state[("st", j - 4)])
            SE.op("copy", whis[sl][:], w32s[sl][:], inc=s_se)
            hi_tok = s_se.tok
            VE.wait(hi_tok)
            if j >= 4:
                VE.wait(spl_state[("st", j - 4)])
            VE.op("tensor_sub", wlos[sl][:], w32s[sl][:], whis[sl][:], inc=s_ve)
            lo_tok = s_ve.tok
            SQ = GP if sl % 2 == 0 else SY
            SQ.wait(hi_tok)
            SQ.wait(lo_tok)
            for tgt_h, tgt_l, c0, c1 in writes:
                SQ.op("dma_start", out=tgt_h, in_=whis[sl][:, c0:c1],
                      inc=spl_st[sl], inc_k=16)
                SQ.op("dma_start", out=tgt_l, in_=wlos[sl][:, c0:c1],
                      inc=spl_st[sl], inc_k=16)
            spl_state[("st", j)] = spl_st[sl].tok
            split_done[0] = tuple(s.tok for s in spl_st)
            for ahead in range(1, 5):
                issue_split_load(j + ahead)

        wr_sems = [sems["Lw32"], sems["Lw32b"], sems["Lw32c"], sems["Lw32d"]]
        cp_tok = {}
        for u in range(2 * NDC):
            sl = u % 4
            dc, hh = divmod(u, 2)
            if u >= 4:
                SY.wait(s_se, cp_tok[u - 4])
            else:
                SY.wait(wr_sems[sl], wr_sems[sl].n)
            SY.op("dma_start", out=wr[sl][:],
                  in_=wfT_in[dc * 128:(dc + 1) * 128, hh * 512:(hh + 1) * 512],
                  inc=wr_sems[sl], inc_k=16)
            SE.wait(wr_sems[sl].tok)
            SE.op("copy", w8[:, u * 512:(u + 1) * 512], wr[sl][:], inc=s_se)
            cp_tok[u] = s_se.n
        w8_done = s_se.n
        SY.wait(s_se, w8_done)
        SY.op("dma_start", out=bias_p0[:], in_=bias_in, inc=s_w32, inc_k=16)
        PE.wait(s_w32.tok)
        PE.wait(s_se, w8_done)
        PE.wait(s_ve, ones_done)

        ck = {}

        def ffn_load(k):
            if k >= NT:
                return
            bi = k % 4
            if ("resid", k - 4) in ck:
                SY.wait(s_ve, ck[("resid", k - 4)])
            if ("y1f_of", k - 4) in ck:
                SY.wait(s_se, ck[("y1f_of", k - 4)])
            SY.op("dma_start", out=xt[bi][:], in_=src_hbm[k * 128:(k + 1) * 128, :],
                  inc=s_x[bi], inc_k=16)
            ck[("xld", k)] = s_x[bi].tok

        def ffn_lnpair(k):
            kA = k + 1 if (k + 1) < NT else None
            kB = (k - 2) if (ln2 and 0 <= (k - 2) < NT) else None
            if kA is not None:
                biA = kA % 4
                VE.wait(ck[("xld", kA)])
                VE.op("bn_stats", st(C_BN, 6), xt[biA][:, 0:512])
                VE.op("bn_stats", st(C_BN + 6, 6), xt[biA][:, 512:1024])
                VE.op("drain")
                VE.op("bn_aggr", st(C_MEAN, 2), st(C_BN, 12), inc=s_ve)
                ck[("aggrA", kA)] = s_ve.tok
            if kB is not None:
                biB = kB % 2
                VE.wait(s_ve, ck[("resid", kB)])
                VE.op("drain")
                VE.op("bn_stats", st(C_BN2, 6), ht[biB][:, 0:512])
                VE.op("bn_stats", st(C_BN2 + 6, 6), ht[biB][:, 512:1024])
                VE.op("drain")
                VE.op("bn_aggr", st(C_MEAN2, 2), st(C_BN2, 12), inc=s_ve)
                ck[("aggrB", kB)] = s_ve.tok
            if kA is not None:
                SE.wait(ck[("aggrA", kA)])
                SE.op("activation", st(C_SQ), st(C_VAR), AF.Sqrt, bias=eps_vec[:],
                      inc=s_se)
                ck[("sqA", kA)] = s_se.tok
            if kB is not None:
                SE.wait(ck[("aggrB", kB)])
                SE.op("activation", st(C_SQ2), st(C_VAR2), AF.Sqrt, bias=eps_vec[:],
                      inc=s_se)
                ck[("sqB", kB)] = s_se.tok
            if kA is not None:
                VE.wait(ck[("sqA", kA)])
                VE.op("reciprocal", st(C_RSTD), st(C_SQ))
                VE.op("drain")
                VE.op("scalar_tensor_tensor", st(C_NEGM), st(C_MEAN), -1.0,
                      st(C_RSTD), ALU.mult, ALU.mult, inc=s_ve)
                ck[("nmA", kA)] = s_ve.tok
            if kB is not None:
                VE.wait(ck[("sqB", kB)])
                VE.op("reciprocal", st(C_RSTD2), st(C_SQ2))
                VE.op("drain")
                VE.op("scalar_tensor_tensor", st(C_NEGM2), st(C_MEAN2), -1.0,
                      st(C_RSTD2), ALU.mult, ALU.mult, inc=s_ve)
                ck[("nmB", kB)] = s_ve.tok
            if kA is not None:
                SE.wait(ck[("nmA", kA)])
                SE.wait(ck[("xld", kA)])
                if ("tr", kA - 2) in ck:
                    SE.wait(ck[("tr", kA - 2)])
                SE.op("activation", y1f[kA % 2][:], xt[biA][:], AF.Identity,
                      bias=st(C_NEGM), scale=st(C_RSTD), inc=s_se)
                ck[("y1f", kA)] = s_se.n
                ck[("y1f_of", kA)] = s_se.n
            if kB is not None:
                SE.wait(ck[("nmB", kB)])
                if ("l2tr", kB - 1) in ck:
                    SE.wait(ck[("l2tr", kB - 1)])
                SE.op("activation", y2f[kB % 2][:], ht[biB][:], AF.Identity,
                      bias=st(C_NEGM2), scale=st(C_RSTD2), inc=s_se)
                ck[("y2f", kB)] = s_se.tok

        def ffn_tr(k):
            if k < 0 or k >= NT:
                return
            bi = k % 2
            pb = bi * 4
            PE.wait(s_se, ck[("y1f", k)])
            PE.wait(sems["Lident"], 16)
            if ("q8", k - 2) in ck:
                PE.wait(s_se, ck[("q8", k - 2)])
            if ln2 and ("l2lo", k - 2) in ck:
                PE.wait(ck[("l2lo", k - 2)])
            for c in range(NDC):
                PE.op("transpose", ps[:, pb * 512 + c * 128: pb * 512 + (c + 1) * 128],
                      y1f[bi][:, c * 128:(c + 1) * 128], ident[:],
                      inc=(s_pe if c == NDC - 1 else None))
            ck[("tr", k)] = s_pe.tok
            SE.wait(ck[("tr", k)])
            if ("mm", k - 2) in ck:
                SE.wait(s_pe, ck[("mm", k - 2)])
            SE.op("activation", y8[bi][:], ps[:, pb * 512:(pb + 2) * 512], AF.Copy,
                  inc=s_se)
            ck[("q8", k)] = s_se.n

        def ffn_mm(k):
            if k < 0 or k >= NT:
                return
            bi = k % 2
            pb = bi * 4
            PE.wait(s_se, ck[("q8", k)])
            if ("f16", k - 2) in ck:
                PE.wait(s_se, ck[("f16", k - 2)])
            for dc in range(NDC):
                PE.op("matmul", bank(pb + 2), y8[bi][:, dc * 128:(dc + 1) * 128],
                      w8[:, dc * D: dc * D + 512], start=(dc == 0), stop=False)
                PE.op("matmul", bank(pb + 3), y8[bi][:, dc * 128:(dc + 1) * 128],
                      w8[:, dc * D + 512: (dc + 1) * D], start=(dc == 0), stop=False)
            PE.op("matmul", bank(pb + 2), ones_row[:], bias_p0[0:1, 0:512],
                  start=False, stop=True)
            PE.op("matmul", bank(pb + 3), ones_row[:], bias_p0[0:1, 512:1024],
                  start=False, stop=True, inc=s_pe)
            ck[("mm", k)] = s_pe.n
            SE.wait(s_pe, ck[("mm", k)])
            if ("t1", k - 2) in ck:
                SE.wait(s_ve, ck[("t1", k - 2)])
            SE.op("copy", f16t[k % 2][:], ps[:, (pb + 2) * 512:(pb + 4) * 512],
                  inc=s_se)
            ck[("f16", k)] = s_se.n

        def ffn_tail(k):
            if k < 0:
                return
            bi = k % 2
            VE.wait(s_se, ck[("f16", k)])
            if ("htst", k - 2) in ck:
                VE.wait(ck[("htst", k - 2)])
            if ("y2f", k - 2) in ck:
                VE.wait(ck[("y2f", k - 2)])
            VE.op("scalar_tensor_tensor", ht[bi][:], f16t[k % 2][:], 0.5, xt[k % 4][:],
                  ALU.mult, ALU.add, inc=s_ve)
            ck[("resid", k)] = s_ve.n
            ck[("t1", k)] = s_ve.n
            GP.wait(s_ve, ck[("resid", k)])
            GP.op("dma_start", out=dst_hbm[k * 128:(k + 1) * 128, :], in_=ht[bi][:],
                  inc=s_sh[bi], inc_k=16)
            ck[("htst", k)] = s_sh[bi].tok

        def ln2_tr(k):
            if not ln2 or k < 0 or k >= NT:
                return
            bi = k % 2
            pb = bi * 4
            PE.wait(ck[("y2f", k)])
            if ("l2lo", k - 2) in ck:
                PE.wait(ck[("l2lo", k - 2)])
            for c in range(NDC):
                PE.op("transpose", ps[:, pb * 512 + c * 128: pb * 512 + (c + 1) * 128],
                      y2f[bi][:, c * 128:(c + 1) * 128], ident[:],
                      inc=(s_pe if c == NDC - 1 else None))
            ck[("l2tr", k)] = s_pe.tok

        def ln2_evac(k):
            if not ln2 or k < 0 or k >= NT:
                return
            sq = k // NTS
            t = k % NTS
            bi = k % 2
            pb = bi * 4
            g4 = t % 4
            dst_h = y2sph[:].rearrange("p (c f) -> p c f", c=NDC)[:, :, g4 * 128:(g4 + 1) * 128]
            dst_l = y2spl[:].rearrange("p (c f) -> p c f", c=NDC)[:, :, g4 * 128:(g4 + 1) * 128]
            SE.wait(ck[("l2tr", k)])
            if ("l2st", k - (4 + g4)) in ck:
                SE.wait(ck[("l2st", k - (4 + g4))])
            SE.op("activation", dst_h, ps[:, pb * 512:(pb + 2) * 512],
                  AF.Copy, inc=s_se)
            hi_done = s_se.tok
            VE.wait(hi_done)
            VE.op("tensor_sub", dst_l, ps[:, pb * 512:(pb + 2) * 512],
                  dst_h, inc=s_ve)
            ck[("l2lo", k)] = s_ve.tok
            if g4 == 3:
                sp0 = (t - 3) // STRIPE
                GP.wait(ck[("l2lo", k)])
                for hs in range(4 // STRIPE):
                    src_h = y2sph[:].rearrange("p (c f) -> p c f", c=NDC)[
                        :, :, hs * SWc:(hs + 1) * SWc]
                    src_l = y2spl[:].rearrange("p (c f) -> p c f", c=NDC)[
                        :, :, hs * SWc:(hs + 1) * SWc]
                    GP.op("dma_start", out=y2h_s[sq, sp0 + hs], in_=src_h,
                          inc=s_sy[k % 2], inc_k=16)
                    GP.op("dma_start", out=y2l_s[sq, sp0 + hs], in_=src_l,
                          inc=s_sy[k % 2], inc_k=16)
                for kk in range(k - 3, k + 1):
                    ck[("l2st", kk)] = s_sy[k % 2].tok

        ffn_load(0)
        ffn_load(1)
        ffn_lnpair(-1)
        ffn_tr(0)
        for k in range(NT):
            ffn_tail(k - 2)
            ffn_load(k + 2)
            ffn_lnpair(k)
            ffn_mm(k)
            ffn_tr(k + 1)
            ln2_tr(k - 2)
            ln2_evac(k - 2)
            if splits:
                emit_split_unit()
                emit_split_unit()
                if k % 2 == 0:
                    emit_split_unit()
        for k in (NT - 2, NT - 1):
            ffn_tail(k)
            ffn_lnpair(k + 2)
            ln2_tr(k)
            ln2_evac(k)
        if splits:
            while spl_state["i"] < len(splits):
                emit_split_unit()
        barrier(skip_stores=("Sw", "Swb", "Swc", "Swd") if skip else ())

    def emit_copy_out(src_hbm):
        ar.phase_begin()
        cp = [ar.phase([128, D], F32) for _ in range(2)]
        s_x = [sems["Lx0"], sems["Lx1"]]
        s_sh = [sems["Sh0"], sems["Sh1"]]
        last_st = {}
        for t in range(NT):
            bi = t % 2
            if t >= 2:
                SY.wait(last_st[t - 2])
            SY.op("dma_start", out=cp[bi][:], in_=src_hbm[t * 128:(t + 1) * 128, :],
                  inc=s_x[bi], inc_k=16)
            GP.wait(s_x[bi].tok)
            GP.op("dma_start", out=out_ext[t * 128:(t + 1) * 128, :], in_=cp[bi][:],
                  inc=s_sh[bi], inc_k=16)
            last_st[t] = s_sh[bi].tok
        barrier()

    emit_ffn(x_in, h_s, ln2=True, splits=_split_units(), skip='Sw')
    if DEBUG_STOP == "ffn1":
        emit_copy_out(h_s)
        _finish(nc, stack, ALL)
        return nc

    for seq in range(SEQ_PER_CORE):
        base = seq * S

        attn_phase_begin()
        y2s2h = [ar.phase([128, NDC * 512], BF16) for _ in range(2)]
        y2s2l = [ar.phase([128, NDC * 512], BF16) for _ in range(2)]
        wvh_b = [ar.phase([128, D], BF16) for _ in range(4)]
        wvl_b = [ar.phase([128, D], BF16) for _ in range(4)]
        vh_e = [ar.phase([128, D], BF16) for _ in range(2)]
        vl_e = [ar.phase([128, D], BF16) for _ in range(2)]
        s_y2sp = sems["Ly2sp"]
        s_wv = [sems["Lwv0"], sems["Lwv1"], sems["Lwv2"], sems["Lwv3"]]
        s_sv = [sems["Sv0"], sems["Sv1"]]
        cons = {}

        def v_span_load(tb):
            if tb >= NTB or ("spl", tb) in cons:
                return
            if tb >= 2:
                SY.wait(s_pe, cons[("vmm", tb - 2)])
            for hs in range(512 // SWc):
                sp_i = tb * (512 // SWc) + hs
                dst_h = y2s2h[tb % 2][:].rearrange("p (c f) -> p c f", c=NDC)[
                    :, :, hs * SWc:(hs + 1) * SWc]
                dst_l = y2s2l[tb % 2][:].rearrange("p (c f) -> p c f", c=NDC)[
                    :, :, hs * SWc:(hs + 1) * SWc]
                SY.op("dma_start", out=dst_h, in_=y2h_s[seq, sp_i], inc=s_y2sp, inc_k=16)
                SY.op("dma_start", out=dst_l, in_=y2l_s[seq, sp_i], inc=s_y2sp, inc_k=16)
            cons[("spl", tb)] = s_y2sp.tok

        for tb in range(NTB):
            v_span_load(tb)
            span_ld = cons[("spl", tb)]
            y2sh, y2sl = y2s2h[tb % 2], y2s2l[tb % 2]
            v_span_load(tb + 1)
            for dc in range(NDC):
                wbi = dc % 4
                if dc >= 4:
                    SY.wait(s_pe, cons[("vdc", tb, dc - 4)])
                elif tb > 0:
                    SY.wait(s_pe, cons[("vdc", tb - 1, dc + 4)])
                if split_done[0] is not None:
                    for _t in split_done[0]:
                        SY.wait(_t)
                SY.op("dma_start", out=wvh_b[wbi][:], in_=wvh_s[dc], inc=s_wv[wbi],
                      inc_k=16)
                SY.op("dma_start", out=wvl_b[wbi][:], in_=wvl_s[dc], inc=s_wv[wbi],
                      inc_k=16)
                wv_ld = s_wv[wbi].tok
                PE.wait(span_ld)
                PE.wait(wv_ld)
                if dc == 0 and tb > 0:
                    PE.wait(s_ve, max(cons[("vlo", tb - 1, _tt)] for _tt in range(4)))
                for tt in range(4):
                    yh = y2sh[:, dc * 512 + tt * 128: dc * 512 + (tt + 1) * 128]
                    yl = y2sl[:, dc * 512 + tt * 128: dc * 512 + (tt + 1) * 128]
                    first, last = dc == 0, dc == NDC - 1
                    for eb in range(2):
                        PE.op("matmul", bank(tt * 2 + eb), yh,
                              wvh_b[wbi][:, eb * 512:(eb + 1) * 512], start=first,
                              stop=False)
                        PE.op("matmul", bank(tt * 2 + eb), yh,
                              wvl_b[wbi][:, eb * 512:(eb + 1) * 512], start=False,
                              stop=False)
                    for eb in range(2):
                        PE.op("matmul", bank(tt * 2 + eb), yl,
                              wvh_b[wbi][:, eb * 512:(eb + 1) * 512], start=False,
                              stop=last,
                              inc=(s_pe if (eb == 1 and tt == 3) else None))
                cons[("vdc", tb, dc)] = s_pe.n
            cons[("vmm", tb)] = v_mm = s_pe.n
            for tt in range(4):
                ebi = tt % 2
                SE.wait(s_pe, v_mm)
                if tt >= 2:
                    SE.wait(cons[("vst", tb, tt - 2)])
                    SE.wait(s_ve, cons[("vlo", tb, tt - 2)])
                SE.op("activation", vh_e[ebi][:], bank(tt * 2, 2), AF.Copy, inc=s_se)
                hi_d = s_se.n
                VE.wait(s_se, hi_d)
                VE.op("tensor_sub", vl_e[ebi][:], bank(tt * 2, 2), vh_e[ebi][:],
                      inc=s_ve)
                cons[("vlo", tb, tt)] = lo_d = s_ve.n
                GP.wait(s_ve, lo_d)
                GP.op("dma_start", out=vh_s[tb * 4 + tt], in_=vh_e[ebi][:],
                      inc=s_sv[ebi], inc_k=16)
                GP.op("dma_start", out=vl_s[tb * 4 + tt], in_=vl_e[ebi][:],
                      inc=s_sv[ebi], inc_k=16)
                cons[("vst", tb, tt)] = s_sv[ebi].tok
        barrier()

        attn_phase_begin()
        y2s2h = [ar.phase([128, NDC * 512], BF16) for _ in range(2)]
        y2s2l = [ar.phase([128, NDC * 512], BF16) for _ in range(2)]
        wkb_h = [ar.phase([128, NDC * 128], BF16) for _ in range(4)]
        wkb_l = [ar.phase([128, NDC * 128], BF16) for _ in range(4)]
        s_wk = [sems["Lwk0"], sems["Lwk1"], sems["Lwk2"], sems["Lwk3"]]
        cons = {}

        def k_span_load(tb):
            if tb >= NTB or ("spl", tb) in cons:
                return
            if tb >= 2:
                SY.wait(s_pe, cons[("kmm", tb - 2, 15)])
            for hs in range(512 // SWc):
                sp_i = tb * (512 // SWc) + hs
                dst_h = y2s2h[tb % 2][:].rearrange("p (c f) -> p c f", c=NDC)[
                    :, :, hs * SWc:(hs + 1) * SWc]
                dst_l = y2s2l[tb % 2][:].rearrange("p (c f) -> p c f", c=NDC)[
                    :, :, hs * SWc:(hs + 1) * SWc]
                SY.op("dma_start", out=dst_h, in_=y2h_s[seq, sp_i], inc=s_y2sp, inc_k=16)
                SY.op("dma_start", out=dst_l, in_=y2l_s[seq, sp_i], inc=s_y2sp, inc_k=16)
            cons[("spl", tb)] = s_y2sp.tok

        for tb in range(NTB):
            k_span_load(tb)
            span_ld = cons[("spl", tb)]
            y2sh, y2sl = y2s2h[tb % 2], y2s2l[tb % 2]
            k_span_load(tb + 1)
            for et in range(16):
                wbi = et % 4
                if et >= 4:
                    SY.wait(s_pe, cons[("kmm", tb, et - 4)])
                elif tb > 0:
                    SY.wait(s_pe, cons[("kmm", tb - 1, et + 12)])
                SY.op("dma_start", out=wkb_h[wbi][:], in_=wkh_s[et], inc=s_wk[wbi],
                      inc_k=16)
                SY.op("dma_start", out=wkb_l[wbi][:], in_=wkl_s[et], inc=s_wk[wbi],
                      inc_k=16)
                wk_ld = s_wk[wbi].tok
                pbk = et % 2
                PE.wait(span_ld)
                PE.wait(wk_ld)
                prev_lo = cons.get(("klo", tb, et - 2)) or \
                    (cons.get(("klo", tb - 1, 14 + et)) if et < 2 else None)
                if prev_lo is not None:
                    PE.wait(prev_lo)
                for dc in range(NDC):
                    wh = wkb_h[wbi][:, dc * 128:(dc + 1) * 128]
                    wl = wkb_l[wbi][:, dc * 128:(dc + 1) * 128]
                    first, last = dc == 0, dc == NDC - 1
                    PE.op("matmul", bank(pbk), wh, y2sh[:, dc * 512:(dc + 1) * 512],
                          start=first, stop=False)
                    PE.op("matmul", bank(pbk), wh, y2sl[:, dc * 512:(dc + 1) * 512],
                          start=False, stop=False)
                    PE.op("matmul", bank(pbk), wl, y2sh[:, dc * 512:(dc + 1) * 512],
                          start=False, stop=last, inc=(s_pe if last else None))
                cons[("kmm", tb, et)] = kmm = s_pe.n
                m = 1 if et < 8 else 2
                col0 = (et % 8) * S + tb * 512
                SE.wait(s_pe, kmm)
                SE.op("activation", k_sb[(m, "h")][:, col0:col0 + 512], bank(pbk),
                      AF.Copy, inc=s_se)
                khi = s_se.n
                VE.wait(s_se, khi)
                VE.op("tensor_sub", k_sb[(m, "l")][:, col0:col0 + 512], bank(pbk),
                      k_sb[(m, "h")][:, col0:col0 + 512], inc=s_ve)
                cons[("klo", tb, et)] = s_ve.tok
        barrier()

        attn_phase_begin()
        SW = 128 * STRIPE
        y2t_mark = ar.mark()
        y2th = ar.phase([128, NDC * SW], BF16)
        y2tl = ar.phase([128, NDC * SW], BF16)
        wqp = [ar.phase([128, D], BF16) for _ in range(6)]
        qtm_e1 = ar.phase([128, max(2 * D, S)], F32)
        qh = ar.phase([128, STRIPE * 2 * D], BF16)
        ql = ar.phase([128, STRIPE * 2 * D], BF16)
        dpp_mark = ar.mark()
        dpp_full = ar.phase([128, max(S, 2 * D)], F32)
        dpp = dpp_full
        dth = ar.at(y2t_mark, [128, S], BF16)
        dtl = ar.at(y2t_mark + S * 2, [128, S], BF16)
        att = ar.phase([128, D], F32)
        vst_h = [ar.phase([128, D], BF16) for _ in range(3)]
        vst_l = [ar.phase([128, D], BF16) for _ in range(3)]
        h2t = ar.phase([128, D], F32)
        q_tm_bufs = [qtm_e1, dpp_full]
        e1 = qtm_e1
        s_y2t, s_ht, s_sh2 = sems["Ly2t"], sems["Lht"], sems["Sh2"]
        s_wq = [sems["Lwq%d" % i] for i in range(6)]
        wq_hist = []
        s_v = [sems["Lv0"], sems["Lv1"], sems["Lv2"]]

        n_stripes = NTS // STRIPE
        tk = {}

        def SUM1(i):
            return st(C_SUM1A + (i % 2))

        def NINV1(i):
            return st(C_NINV1A + (i % 2))

        def emit_scores(i, mat):
            tt = i % STRIPE
            b0 = (mat - 1) * 4
            if mat == 1:
                PE.wait(tk[("qse", i)])
                PE.wait(tk[("qve", i)])
                if ("exp1", i - 1) in tk:
                    PE.wait(tk[("exp1", i - 1)])
            else:
                g_last = i - (i % STRIPE) + STRIPE - 1
                PE.wait(tk[("qse", g_last)])
                PE.wait(tk[("qve", g_last)])
                if ("dtev", i - 1) in tk:
                    PE.wait(tk[("dtes", i - 1)])
                    PE.wait(tk[("dtev", i - 1)])
                if ("attev", i - 1) in tk:
                    PE.wait(tk[("attev", i - 1)])
            kh = k_sb[(mat, "h")]
            kl = k_sb[(mat, "l")]
            for tb in range(NTB):
                for ec in range(NDC):
                    gec = (mat - 1) * 8 + ec
                    qh_sl = qh[:, tt * 2048 + gec * 128: tt * 2048 + (gec + 1) * 128]
                    ql_sl = ql[:, tt * 2048 + gec * 128: tt * 2048 + (gec + 1) * 128]
                    first, last = ec == 0, ec == NDC - 1
                    ksl = slice(ec * S + tb * 512, ec * S + (tb + 1) * 512)
                    PE.op("matmul", bank(b0 + tb), qh_sl, kh[:, ksl],
                          start=first, stop=False)
                    PE.op("matmul", bank(b0 + tb), qh_sl, kl[:, ksl],
                          start=False, stop=False)
                    PE.op("matmul", bank(b0 + tb), ql_sl, kh[:, ksl],
                          start=False, stop=last,
                          inc=(s_pe if last else None))
                tk[("s%d_b%d" % (mat, tb), i)] = s_pe.tok
            tk[("s%d" % mat, i)] = s_pe.tok

        def emit_max(i, mat):
            b0 = (mat - 1) * 4
            VE.op("drain")
            for tb in range(NTB):
                VE.wait(tk[("s%d_b%d" % (mat, tb), i)])
                VE.op("reduce_max", st(C_MB + tb), bank(b0 + tb), axis=AX.X)
            VE.op("drain")
            nms = C_NMS1 if mat == 1 else C_NMS2
            if NTB == 4:
                VE.op("tensor_max", st(C_MA), st(C_MB), st(C_MB + 1))
                VE.op("tensor_max", st(C_MBB), st(C_MB + 2), st(C_MB + 3))
                VE.op("drain")
                VE.op("tensor_scalar", st(nms), st(C_MA), st(C_MBB), -SCALE,
                      ALU.max, ALU.mult, inc=s_ve)
            else:
                VE.op("tensor_scalar_mul", st(nms), st(C_MB), -SCALE, inc=s_ve)
            tk[("nms%d" % mat, i)] = s_ve.tok

        def emit_exp1(i):
            SE.wait(tk[("s1", i)])
            SE.wait(tk[("nms1", i)])
            SE.op("activation", e1[:, 0:S], ps[:, 0:S], AF.Exp,
                  bias=st(C_NMS1), scale=SCALE, accum_out=SUM1(i), inc=s_se)
            tk[("exp1", i)] = s_se.tok

        def emit_exp2_dpp(i):
            SE.wait(tk[("s2", i)])
            SE.wait(tk[("nms2", i)])
            SE.op("activation", ps[:, 2048:2048 + S], ps[:, 2048:2048 + S], AF.Exp,
                  bias=st(C_NMS2), scale=SCALE, accum_out=st(C_SUM2), inc=s_se)
            tk[("exp2", i)] = s_se.tok
            VE.wait(tk[("exp1", i)])
            VE.op("reciprocal", NINV1(i), SUM1(i))
            VE.op("drain")
            VE.op("tensor_scalar_mul", NINV1(i), NINV1(i), -1.0)
            VE.wait(tk[("exp2", i)])
            VE.op("reciprocal", st(C_R2), st(C_SUM2))
            VE.op("drain")
            VE.op("scalar_tensor_tensor", st(C_C), SUM1(i), lam_vec[:], st(C_R2),
                  ALU.mult, ALU.mult)
            VE.op("drain")
            HS_ = S // 2
            VE.op("scalar_tensor_tensor", dpp[:, 0:HS_], ps[:, 2048:2048 + HS_],
                  st(C_C), e1[:, 0:HS_], ALU.mult, ALU.subtract, inc=s_ve)
            tk[("dpp0", i)] = s_ve.tok
            VE.op("scalar_tensor_tensor", dpp[:, HS_:S], ps[:, 2048 + HS_:2048 + S],
                  st(C_C), e1[:, HS_:S], ALU.mult, ALU.subtract, inc=s_ve)
            tk[("dpp", i)] = s_ve.tok

        def emit_trd_evac(i):
            PE.wait(tk[("dpp0", i)])
            half = NTS // 2
            for c in range(NTS):
                if c == half:
                    PE.wait(tk[("dpp", i)])
                PE.op("transpose", ps[:, 2048 + c * 128: 2048 + (c + 1) * 128],
                      dpp[:, c * 128:(c + 1) * 128], ident[:],
                      inc=(s_pe if c in (half - 1, NTS - 1) else None))
                if c == half - 1:
                    tk[("trd0", i)] = s_pe.tok
            tk[("trd1", i)] = s_pe.tok
            HS = S // 2
            SE.wait(tk[("trd0", i)])
            SE.op("activation", dth[:, 0:HS], ps[:, 2048:2048 + HS], AF.Copy, inc=s_se)
            tk[("dth0", i)] = s_se.tok
            VE.wait(tk[("dth0", i)])
            VE.op("tensor_sub", dtl[:, 0:HS], ps[:, 2048:2048 + HS], dth[:, 0:HS], inc=s_ve)
            tk[("dtl0", i)] = s_ve.tok
            SE.wait(tk[("trd1", i)])
            SE.op("activation", dth[:, HS:S], ps[:, 2048 + HS:2048 + S], AF.Copy, inc=s_se)
            tk[("dtes", i)] = s_se.tok
            VE.wait(tk[("dtes", i)])
            VE.op("tensor_sub", dtl[:, HS:S], ps[:, 2048 + HS:2048 + S], dth[:, HS:S],
                  inc=s_ve)
            tk[("dtev", i)] = s_ve.tok

        def emit_pv(i, seq_base):
            PE.wait(tk[("dth0", i)])
            PE.wait(tk[("dtl0", i)])
            if S < 2048:
                PE.wait(tk[("dtes", i)])
                PE.wait(tk[("dtev", i)])
            for tc in range(NTS):
                vbi = tc % 3
                if tc >= 3:
                    prev = tk.get(("pvc", i, tc - 3))
                else:
                    prev_c = max(c for c in range(NTS) if c % 3 == vbi)
                    prev = tk.get(("pvc", i - 1, prev_c))
                if prev is not None:
                    SY.wait(prev)
                SY.op("dma_start", out=vst_h[vbi][:], in_=vh_s[tc], inc=s_v[vbi],
                      inc_k=16)
                SY.op("dma_start", out=vst_l[vbi][:], in_=vl_s[tc], inc=s_v[vbi],
                      inc_k=16)
                PE.wait(s_v[vbi].tok)
                if tc == NTS // 2:
                    PE.wait(tk[("dtes", i)])
                    PE.wait(tk[("dtev", i)])
                first, last = tc == 0, tc == NTS - 1
                for eb in range(2):
                    PE.op("matmul", bank(4 + eb), dth[:, tc * 128:(tc + 1) * 128],
                          vst_h[vbi][:, eb * 512:(eb + 1) * 512], start=first, stop=False)
                    PE.op("matmul", bank(4 + eb), dth[:, tc * 128:(tc + 1) * 128],
                          vst_l[vbi][:, eb * 512:(eb + 1) * 512], start=False, stop=False)
                for eb in range(2):
                    PE.op("matmul", bank(4 + eb), dtl[:, tc * 128:(tc + 1) * 128],
                          vst_h[vbi][:, eb * 512:(eb + 1) * 512], start=False,
                          stop=last, inc=(s_pe if eb == 1 else None))
                tk[("pvc", i, tc)] = s_pe.tok
            tk[("pv", i)] = s_pe.tok

        def emit_epilogueA(i, seq_base):
            SE.wait(tk[("pv", i)])
            if ("h2", i - 1) in tk:
                SE.wait(tk[("h2", i - 1)])
            if ("h2st", i - 1) in tk:
                SE.wait(tk[("h2st", i - 1)])
            SE.op("activation", att[:], bank(4, 2), AF.Copy, scale=NINV1(i))
            SE.op("activation", h2t[:], bank(4, 2), AF.Square, scale=NINV1(i),
                  accum_out=st(C_SSQ), inc=s_se)
            tk[("attev", i)] = s_se.tok

        def emit_epilogueB(i, seq_base):
            if i < seq * NTS or i >= (seq + 1) * NTS:
                return
            qt = i % NTS
            if ("h2st", i - 1) in tk:
                SY.wait(tk[("h2st", i - 1)])
            SY.wait(tk[("attev", i)])
            SY.op("dma_start", out=h2t[:],
                  in_=h_s[seq_base + qt * 128: seq_base + qt * 128 + 128, :],
                  inc=s_ht, inc_k=16)
            ht_ld = s_ht.tok
            SE.op("drain")
            SE.op("activation", st(C_RMS), st(C_SSQ), AF.Sqrt, bias=eps_vec[:],
                  scale=float(1.0 / D), inc=s_se)
            rms_done = s_se.tok
            VE.wait(rms_done)
            VE.op("reciprocal", st(C_FS), st(C_RMS))
            VE.op("drain")
            VE.op("tensor_scalar_mul", st(C_FS), st(C_FS), float(OUT_MUL))
            VE.op("drain")
            VE.wait(ht_ld)
            VE.op("scalar_tensor_tensor", h2t[:], att[:], st(C_FS), h2t[:],
                  ALU.mult, ALU.add, inc=s_ve)
            tk[("h2", i)] = s_ve.tok
            GP.wait(tk[("h2", i)])
            GP.op("dma_start", out=h2_s[seq_base + qt * 128: seq_base + qt * 128 + 128, :],
                  in_=h2t[:], inc=s_sh2, inc_k=16)
            tk[("h2st", i)] = s_sh2.tok

        for sp in range(n_stripes):
            t0 = sp * STRIPE
            ga = seq * NTS + t0
            gb = ga + 1
            if ("pv", ga - 1) in tk:
                SE.wait(tk[("pv", ga - 1)])
            SE.op("dma_start", out=y2th[:], in_=y2h_s[seq, t0 // STRIPE],
                  inc=s_y2t, inc_k=16)
            SE.op("dma_start", out=y2tl[:], in_=y2l_s[seq, t0 // STRIPE],
                  inc=s_y2t, inc_k=16)
            y2t_ld = s_y2t.tok
            if ("exp1", ga - 1) in tk:
                PE.wait(tk[("exp1", ga - 1)])
            if ("attev", ga - 1) in tk:
                PE.wait(tk[("attev", ga - 1)])
            if ("dtev", ga - 1) in tk:
                PE.wait(tk[("dtes", ga - 1)])
                PE.wait(tk[("dtev", ga - 1)])
            for dc in range(NDC):
                halves = [(wqh_s[dc, :, 0:D], "h0", (dc % 2) * 2),
                          (wqh_s[dc, :, D:2 * D], "h1", (dc % 2) * 2 + 1),
                          (wql_s[dc, :, 0:D], "l0", 4),
                          (wql_s[dc, :, D:2 * D], "l1", 5)]
                sl = {}
                for src_ap_, nm_, j in halves:
                    depth = 2 if nm_[0] == "h" else 1
                    if len(wq_hist) >= depth:
                        SE.wait(wq_hist[-depth])
                    SE.op("dma_start", out=wqp[j][:], in_=src_ap_, inc=s_wq[j],
                          inc_k=16)
                    sl[nm_] = (j, s_wq[j].tok)
                PE.wait(y2t_ld)
                first, last = dc == 0, dc == NDC - 1
                for tt in range(STRIPE):
                    yh = y2th[:, dc * SW + tt * 128: dc * SW + (tt + 1) * 128]
                    for eb in range(4):
                        hs, htok = sl["h0"] if eb < 2 else sl["h1"]
                        PE.wait(htok)
                        PE.op("matmul", bank(tt * 4 + eb), yh,
                              wqp[hs][:, (eb % 2) * 512:(eb % 2 + 1) * 512],
                              start=first, stop=False)
                for tt in range(STRIPE):
                    yl = y2tl[:, dc * SW + tt * 128: dc * SW + (tt + 1) * 128]
                    for eb in range(4):
                        hs, htok = sl["h0"] if eb < 2 else sl["h1"]
                        PE.op("matmul", bank(tt * 4 + eb), yl,
                              wqp[hs][:, (eb % 2) * 512:(eb % 2 + 1) * 512],
                              start=False, stop=False)
                for tt in range(STRIPE):
                    yh = y2th[:, dc * SW + tt * 128: dc * SW + (tt + 1) * 128]
                    for eb in range(4):
                        ls, ltok = sl["l0"] if eb < 2 else sl["l1"]
                        PE.wait(ltok)
                        PE.op("matmul", bank(tt * 4 + eb), yh,
                              wqp[ls][:, (eb % 2) * 512:(eb % 2 + 1) * 512],
                              start=False, stop=last,
                              inc=(s_pe if (eb == 3 and tt == STRIPE - 1) else None))
                wq_hist.append(s_pe.tok)
            tk[("qpmm", seq * 100 + sp)] = s_pe.tok
            tk[("qpmm_last",)] = s_pe.tok
            qp_mm = s_pe.tok
            for tt in range(STRIPE):
                g = ga + tt
                q_tm = q_tm_bufs[tt % 2]
                VE.wait(qp_mm)
                VE.op("tensor_copy", q_tm[:, 0:2 * D], bank(tt * 4, 4), inc=s_ve)
                qtm_done = s_ve.tok
                PE.wait(qtm_done)
                for c in range(16):
                    PE.op("transpose",
                          ps[:, tt * 4 * 512 + c * 128: tt * 4 * 512 + (c + 1) * 128],
                          q_tm[:, c * 128:(c + 1) * 128], ident[:],
                          inc=(s_pe if c == 15 else None))
                tk[("qtr", g)] = s_pe.tok
                SE.wait(tk[("qtr", g)])
                SE.op("activation", qh[:, tt * 2048:(tt + 1) * 2048], bank(tt * 4, 4),
                      AF.Copy, inc=s_se)
                tk[("qse", g)] = s_se.tok
                VE.wait(tk[("qse", g)])
                VE.op("tensor_sub", ql[:, tt * 2048:(tt + 1) * 2048], bank(tt * 4, 4),
                      qh[:, tt * 2048:(tt + 1) * 2048], inc=s_ve)
                tk[("qve", g)] = s_ve.tok

            emit_scores(ga, 1)
            emit_max(ga, 1)
            emit_exp1(ga)
            emit_scores(ga, 2)
            emit_max(ga, 2)
            emit_exp2_dpp(ga)
            emit_scores(gb, 1)
            emit_max(gb, 1)
            emit_epilogueB(ga - 1, base)
            emit_trd_evac(ga)
            emit_exp1(gb)
            emit_pv(ga, base)
            emit_epilogueA(ga, base)
            emit_scores(gb, 2)
            emit_max(gb, 2)
            emit_exp2_dpp(gb)
            emit_epilogueB(ga, base)
            emit_trd_evac(gb)
            emit_pv(gb, base)
            emit_epilogueA(gb, base)
        emit_epilogueB(seq * NTS + NTS - 1, base)
        barrier()

    if DEBUG_STOP == "attn":
        emit_copy_out(h2_s)
        _finish(nc, stack, ALL)
        return nc

    emit_ffn(h2_s, out_ext)

    _finish(nc, stack, ALL)
    return nc


def _finish(nc, stack, ALL):
    blk = stack.enter_context(nc.Block())
    blk.sync(ALL["sync"].replay)
    blk.tensor(ALL["tensor"].replay)
    blk.vector(ALL["vector"].replay)
    blk.scalar(ALL["scalar"].replay)
    blk.gpsimd(ALL["gpsimd"].replay)
    stack.close()


def kernel(L0_attn_x, L0_attn_w_q12, L0_attn_w_k12, L0_attn_w_v,
           L0_attn_lambda_q1, L0_attn_lambda_k1, L0_attn_lambda_q2,
           L0_attn_lambda_k2, L0_ffn_weight, L0_ffn_bias):
    x = np.ascontiguousarray(np.asarray(L0_attn_x, dtype=np.float32))
    wq = np.ascontiguousarray(np.asarray(L0_attn_w_q12, dtype=np.float32))
    wk = np.ascontiguousarray(np.asarray(L0_attn_w_k12, dtype=np.float32))
    wv = np.ascontiguousarray(np.asarray(L0_attn_w_v, dtype=np.float32))
    wfT = np.ascontiguousarray(np.asarray(L0_ffn_weight, dtype=np.float32).T)
    bias = np.ascontiguousarray(np.asarray(L0_ffn_bias, dtype=np.float32).reshape(1, D))
    lams = [np.ascontiguousarray(np.asarray(v, dtype=np.float32).reshape(128, 8))
            for v in (L0_attn_lambda_q1, L0_attn_lambda_k1,
                      L0_attn_lambda_q2, L0_attn_lambda_k2)]

    nc = build_nc()
    in_maps = []
    for c in range(NCORES):
        xc = np.ascontiguousarray(
            x[c * SEQ_PER_CORE:(c + 1) * SEQ_PER_CORE].reshape(T, D))
        in_maps.append({"x": xc, "wq": wq, "wk": wk, "wv": wv, "wfT": wfT,
                        "bias": bias, "lq1": lams[0], "lk1": lams[1],
                        "lq2": lams[2], "lk2": lams[3]})
    res = run_bass_kernel_spmd(nc, in_maps, core_ids=list(range(NCORES)))
    out = np.empty((B, S, D), dtype=np.float32)
    for c in range(NCORES):
        out[c * SEQ_PER_CORE:(c + 1) * SEQ_PER_CORE] = \
            res.results[c]["out"].reshape(SEQ_PER_CORE, S, D)
    return out


# version 36
# speedup vs baseline: 1.0769x; 1.0769x over previous
import numpy as np
import concourse.bass as bass
import concourse.mybir as mybir
from concourse.bass_utils import run_bass_kernel_spmd

F32 = mybir.dt.float32
BF16 = mybir.dt.bfloat16
F16 = mybir.dt.float16
F8 = mybir.dt.float8e4
U8 = mybir.dt.uint8
AF = mybir.ActivationFunctionType
ALU = mybir.AluOpType
AX = mybir.AxisListType

D = 1024
NDC = D // 128
SCALE = float(D ** (-0.25))
LAMBDA_INIT = 0.8
EPS = 1e-5
OUT_MUL = 1.0 - LAMBDA_INIT
STRIPE = 2
DEBUG_STOP = None

B = S = NCORES = SEQ_PER_CORE = T = NT = NTS = NTB = None


def _set_dims(b, s, ncores):
    global B, S, NCORES, SEQ_PER_CORE, T, NT, NTS, NTB
    B, S, NCORES = b, s, ncores
    SEQ_PER_CORE = B // NCORES
    T = SEQ_PER_CORE * S
    NT = T // 128
    NTS = S // 128
    NTB = S // 512
    assert S % 512 == 0 and NTS % STRIPE == 0


_set_dims(16, 2048, 8)


class Sem:
    def __init__(self, handle, name):
        self.h = handle
        self.name = name
        self.n = 0

    def bump(self, k=1):
        self.n += k
        return self.n

    @property
    def tok(self):
        return (self, self.n)


class ES:
    def __init__(self, name):
        self.name = name
        self.ops = []
        self._waited = {}

    def wait(self, sem, val=None):
        if isinstance(sem, tuple):
            sem, val = sem
        if val is None:
            val = sem.n
        if val <= 0:
            return
        if self._waited.get(sem.name, -1) >= val:
            return
        self._waited[sem.name] = val
        self.ops.append(("w", sem, val))

    def op(self, meth, *args, inc=None, inc_k=1, **kw):
        if inc is not None:
            inc.bump(inc_k)
            self.ops.append(("o", meth, args, kw, inc, inc_k))
            return inc.n
        self.ops.append(("o", meth, args, kw, None, 0))
        return None

    def replay(self, eng):
        for item in self.ops:
            if item[0] == "w":
                eng.wait_ge(item[1].h, item[2])
            else:
                _, meth, args, kw, inc, k = item
                instr = getattr(eng, meth)(*args, **kw)
                if inc is not None:
                    instr.then_inc(inc.h, k)


class Arena:
    def __init__(self, nc, total_bytes):
        self.nc = nc
        self.guard = nc.sbuf_tensor("arena", [128, total_bytes], U8)
        self.h = self.guard.__enter__()
        self.base = nc.lookup_mloc(self.h).addr
        self.per_off = 0
        self.ph_off = None
        self.total = total_bytes
        self.uid = 0

    def _alloc(self, shape, dtype, off):
        per_part = int(np.prod(shape[1:])) * mybir.dt.size(dtype)
        off_al = (off + 31) // 32 * 32
        name = f"ar{self.uid}"
        self.uid += 1
        hnd = self.nc.virtual_sbuf_tensor(name, shape, dtype)
        self.nc._tensor(hnd.name, list(shape), dtype, type="SB")
        mloc = self.nc.lookup_mloc(hnd)
        mloc.allocated = True
        mloc.addr = self.base + off_al
        return hnd, off_al + per_part

    def persist(self, shape, dtype):
        h, end = self._alloc(shape, dtype, self.per_off)
        self.per_off = end
        assert self.per_off <= self.total, f"persist overflow {self.per_off}"
        return h

    def phase_begin(self):
        self.ph_off = self.per_off

    def mark(self):
        return self.ph_off

    def at(self, off, shape, dtype):
        h, end = self._alloc(shape, dtype, off)
        assert end <= self.total
        return h

    def phase(self, shape, dtype):
        h, end = self._alloc(shape, dtype, self.ph_off)
        self.ph_off = end
        assert self.ph_off <= self.total, \
            f"phase overflow {self.ph_off} > {self.total}"
        return h


def build_nc():
    nc = bass.Bass("TRN2", target_bir_lowering=False)

    x_in = nc.dram_tensor("x", [T, D], F32, kind="ExternalInput").ap()
    wq_in = nc.dram_tensor("wq", [D, 2 * D], F32, kind="ExternalInput").ap()
    wk_in = nc.dram_tensor("wk", [D, 2 * D], F32, kind="ExternalInput").ap()
    wv_in = nc.dram_tensor("wv", [D, D], F32, kind="ExternalInput").ap()
    wfT_in = nc.dram_tensor("wfT", [D, D], F32, kind="ExternalInput").ap()
    bias_in = nc.dram_tensor("bias", [1, D], F32, kind="ExternalInput").ap()
    lq1_in = nc.dram_tensor("lq1", [128, 8], F32, kind="ExternalInput").ap()
    lk1_in = nc.dram_tensor("lk1", [128, 8], F32, kind="ExternalInput").ap()
    lq2_in = nc.dram_tensor("lq2", [128, 8], F32, kind="ExternalInput").ap()
    lk2_in = nc.dram_tensor("lk2", [128, 8], F32, kind="ExternalInput").ap()
    out_ext = nc.dram_tensor("out", [T, D], F32, kind="ExternalOutput").ap()

    h_s = nc.dram_tensor("h_s", [T, D], F32, kind="Internal").ap()
    h2_s = nc.dram_tensor("h2_s", [T, D], F32, kind="Internal").ap()
    SWc = 128 * STRIPE
    NSP = S // SWc
    y2h_s = nc.dram_tensor("y2h_s", [SEQ_PER_CORE, NSP, 128, NDC * SWc], BF16, kind="Internal").ap()
    y2l_s = nc.dram_tensor("y2l_s", [SEQ_PER_CORE, NSP, 128, NDC * SWc], BF16, kind="Internal").ap()
    vh_s = nc.dram_tensor("vh_s", [NTS, 128, D], BF16, kind="Internal").ap()
    vl_s = nc.dram_tensor("vl_s", [NTS, 128, D], BF16, kind="Internal").ap()
    wqh_s = nc.dram_tensor("wqh_s", [NDC, 128, 2 * D], BF16, kind="Internal").ap()
    wql_s = nc.dram_tensor("wql_s", [NDC, 128, 2 * D], BF16, kind="Internal").ap()
    wkh_s = nc.dram_tensor("wkh_s", [16, 128, NDC * 128], BF16, kind="Internal").ap()
    wkl_s = nc.dram_tensor("wkl_s", [16, 128, NDC * 128], BF16, kind="Internal").ap()
    wvh_s = nc.dram_tensor("wvh_s", [NDC, 128, D], BF16, kind="Internal").ap()
    wvl_s = nc.dram_tensor("wvl_s", [NDC, 128, D], BF16, kind="Internal").ap()

    import contextlib
    stack = contextlib.ExitStack()
    E = stack.enter_context

    ps = E(nc.psum_tensor("ps", [128, 4096], F32))

    def bank(b, n=1):
        return ps[:, b * 512:(b + n) * 512]

    sem_names = ["pe", "ve", "se",
                 "b_sy", "b_pe", "b_ve", "b_se", "b_gp",
                 "Lident", "Lli", "Lw32", "Lx0", "Lx1", "Ly2sp", "Lwv0", "Lwv1", "Lwv2", "Lwv3",
                 "Lwk0", "Lwk1", "Lwk2", "Lwk3", "Ly2t", "Lwq0", "Lwq1", "Lwq2", "Lwq3", "Lwq4", "Lwq5", "Lv0", "Lv1", "Lv2", "Lht", "Lw32b", "Lw32c", "Lw32d", "Lx2", "Lx3", "Lx4",
                 "Sw", "Swb", "Swc", "Swd", "Sh0", "Sh1", "Sy0", "Sy1", "Sv0", "Sv1", "Sh2"]
    sems = {nm: Sem(E(nc.semaphore("sem_" + nm)), nm) for nm in sem_names}
    s_pe, s_ve, s_se = sems["pe"], sems["ve"], sems["se"]
    store_sems = [sems[nm] for nm in
                  ["Sw", "Swb", "Swc", "Swd", "Sh0", "Sh1", "Sy0", "Sy1", "Sv0",
                   "Sv1", "Sh2"]]

    SY, PE, VE, SE, GP = ES("sync"), ES("tensor"), ES("vector"), ES("scalar"), ES("gpsimd")
    ALL = {"sync": SY, "tensor": PE, "vector": VE, "scalar": SE, "gpsimd": GP}
    bar_sem = {"sync": sems["b_sy"], "tensor": sems["b_pe"], "vector": sems["b_ve"],
               "scalar": sems["b_se"], "gpsimd": sems["b_gp"]}
    bar_count = [0]

    def barrier(skip_stores=()):
        bar_count[0] += 1
        k = bar_count[0]
        for ssem in store_sems:
            if ssem.name not in skip_stores:
                GP.wait(ssem, ssem.n)
        for nm, es in ALL.items():
            es.op("drain", inc=bar_sem[nm])
        for nm, es in ALL.items():
            for nm2, sem2 in bar_sem.items():
                if nm2 != nm:
                    es.wait(sem2, k)

    AR_BYTES = 212832
    ar = Arena(nc, AR_BYTES)

    ident = ar.persist([128, 128], F32)
    ones_col = ar.persist([128, 1], F32)
    ones_row = ar.persist([1, 128], F32)
    lam_vec = ar.persist([128, 1], F32)
    stat = ar.persist([128, 64], F32)
    sml = ar.persist([1, 16], F32)
    eps_vec = ar.persist([128, 1], F32)
    bias_p0 = ar.persist([1, D], F32)

    attn_base = (ar.per_off + 31) // 32 * 32
    k_sb = {}
    _o = attn_base
    for m in (1, 2):
        for p in ("h", "l"):
            k_sb[(m, p)] = ar.at(_o, [128, NDC * S], BF16)
            _o += NDC * S * 2
    ATTN_PH = _o

    def attn_phase_begin():
        ar.ph_off = ATTN_PH

    ident_h = nc.inline_tensor(np.eye(128, dtype=np.float32), name="identity_c")

    (C_BN, C_MEAN, C_VAR, C_SQ, C_RSTD, C_NEGM,
     C_MB, C_MA, C_MBB, C_NMS1, C_NMS2, C_SUM1A, C_SUM1B, C_SUM2,
     C_R2, C_C, C_NINV1A, C_NINV1B, C_SSQ, C_RMS, C_FS) = \
        (0, 12, 13, 14, 15, 16, 17, 21, 22, 23, 24, 25, 26, 27, 28, 29, 30, 31, 32, 33, 34)
    (C_BN2, C_MEAN2, C_VAR2, C_SQ2, C_RSTD2, C_NEGM2) = (36, 48, 49, 50, 51, 52)

    def st(c, n=1):
        return stat[:, c:c + n]

    ar.phase_begin()
    lam_bufs = [ar.phase([128, 8], F32) for _ in range(4)]

    s_li, s_w32, s_sw = sems["Lli"], sems["Lw32"], sems["Sw"]
    SY.op("dma_start", out=ident[:], in_=ident_h.ap(), inc=sems["Lident"], inc_k=16)
    for i, tin in enumerate([lq1_in, lk1_in, lq2_in, lk2_in]):
        SY.op("dma_start", out=lam_bufs[i][:], in_=tin, inc=s_li, inc_k=16)
    lam_ld = s_li.tok
    VE.op("memset", ones_col[:], 1.0)
    VE.op("memset", ones_row[:], 1.0)
    VE.op("memset", eps_vec[:], float(EPS))
    VE.op("drain", inc=s_ve)
    ones_done = s_ve.n
    VE.wait(lam_ld)
    VE.op("tensor_mul", lam_bufs[0][:], lam_bufs[0][:], lam_bufs[1][:])
    VE.op("tensor_mul", lam_bufs[2][:], lam_bufs[2][:], lam_bufs[3][:])
    VE.op("drain")
    VE.op("reduce_sum", lam_bufs[1][:, 0:1], lam_bufs[0][:], axis=AX.X)
    VE.op("reduce_sum", lam_bufs[3][:, 0:1], lam_bufs[2][:], axis=AX.X)
    VE.op("drain", inc=s_ve)
    lam_red = s_ve.n
    PE.wait(s_ve, lam_red)
    PE.op("matmul", ps[0:1, 0:1], ones_col[:], lam_bufs[1][:, 0:1], start=True, stop=True)
    PE.op("matmul", ps[0:1, 512:513], ones_col[:], lam_bufs[3][:, 0:1], start=True,
          stop=True, inc=s_pe)
    lam_mm = s_pe.n
    SE.wait(s_pe, lam_mm)
    SE.op("activation", sml[:, 0:1], ps[0:1, 0:1], AF.Exp)
    SE.op("activation", sml[:, 1:2], ps[0:1, 512:513], AF.Exp)
    SE.op("drain", inc=s_se)
    lam_exp = s_se.n
    VE.wait(s_se, lam_exp)
    VE.op("tensor_sub", sml[:, 2:3], sml[:, 0:1], sml[:, 1:2])
    VE.op("drain")
    VE.op("tensor_scalar_add", sml[:, 2:3], sml[:, 2:3], float(LAMBDA_INIT))
    VE.op("drain", inc=s_ve)
    lam_f = s_ve.n
    PE.wait(s_ve, lam_f)
    PE.wait(s_ve, ones_done)
    PE.op("matmul", ps[:, 1024:1025], ones_row[:], sml[:, 2:3], start=True, stop=True,
          inc=s_pe)
    lam_b = s_pe.n
    SE.wait(s_pe, lam_b)
    SE.op("copy", lam_vec[:], ps[:, 1024:1025], inc=s_se)

    split_done = [None]
    def _split_units():
        units = []
        for dc in range(NDC):
            r0, r1 = dc * 128, (dc + 1) * 128
            for q4 in range(2):
                units.append((wv_in[r0:r1, q4 * 512:(q4 + 1) * 512],
                              [(wvh_s[dc, :, q4 * 512:(q4 + 1) * 512],
                                wvl_s[dc, :, q4 * 512:(q4 + 1) * 512], 0, 512)]))
        for dc in range(NDC):
            r0, r1 = dc * 128, (dc + 1) * 128
            for q4 in range(4):
                units.append((wk_in[r0:r1, q4 * 512:(q4 + 1) * 512],
                              [(wkh_s[et, :, dc * 128:(dc + 1) * 128],
                                wkl_s[et, :, dc * 128:(dc + 1) * 128],
                                (et - q4 * 4) * 128, (et - q4 * 4 + 1) * 128)
                               for et in range(q4 * 4, q4 * 4 + 4)]))
        for dc in range(NDC):
            r0, r1 = dc * 128, (dc + 1) * 128
            for q4 in range(4):
                units.append((wq_in[r0:r1, q4 * 512:(q4 + 1) * 512],
                              [(wqh_s[dc, :, q4 * 512:(q4 + 1) * 512],
                                wql_s[dc, :, q4 * 512:(q4 + 1) * 512], 0, 512)]))
        return units

    def emit_ffn(src_hbm, dst_hbm, ln2=False, splits=None, skip=None):
        ar.phase_begin()
        w8 = ar.phase([128, NDC * D], F8)
        xt = [ar.phase([128, D], F32) for _ in range(5)]
        y1f = [ar.phase([128, D], F32) for _ in range(2)]
        y8 = [ar.phase([128, D], F8) for _ in range(2)]
        f16t = [ar.phase([128, D], F16) for _ in range(2)]
        ht = [ar.phase([128, D], F32) for _ in range(2)]
        s_x = [sems["Lx%d" % i] for i in range(5)]
        s_sh = [sems["Sh0"], sems["Sh1"]]
        if ln2:
            y2f_1 = ar.phase([128, D], F32)
            y2f = [y2f_1, y2f_1]
            y2sph = ar.phase([128, NDC * 512], BF16)
            y2spl = ar.phase([128, NDC * 512], BF16)
            s_sy = [sems["Sy0"], sems["Sy1"]]
        if splits:
            w32s = [ar.phase([128, 512], F32) for _ in range(4)]
            whis = [ar.phase([128, 512], BF16) for _ in range(4)]
            wlos = [ar.phase([128, 512], BF16) for _ in range(4)]
            spl_state = {"i": 0}
            spl_sems = [sems["Lw32"], sems["Lw32b"], sems["Lw32c"], sems["Lw32d"]]
            spl_st = [sems["Sw"], sems["Swb"], sems["Swc"], sems["Swd"]]
            wr = w32s
        else:
            wr = [ar.phase([128, 512], F32) for _ in range(4)]

        def issue_split_load(j):
            if j >= len(splits) or ("ld", j) in spl_state:
                return
            sl = j % 4
            src_ap_, _ = splits[j]
            if j >= 4:
                SY.wait(spl_state[("st", j - 4)])
            else:
                SY.wait(spl_sems[sl], spl_sems[sl].n)
                SY.wait(s_se, s_se.n)
            SY.op("dma_start", out=w32s[sl][:], in_=src_ap_, inc=spl_sems[sl],
                  inc_k=16)
            spl_state[("ld", j)] = spl_sems[sl].tok

        def emit_split_unit():
            j = spl_state["i"]
            if j >= len(splits):
                return
            spl_state["i"] += 1
            sl = j % 4
            _, writes = splits[j]
            issue_split_load(j)
            SE.wait(spl_state[("ld", j)])
            if j >= 4:
                SE.wait(spl_state[("st", j - 4)])
            SE.op("copy", whis[sl][:], w32s[sl][:], inc=s_se)
            hi_tok = s_se.tok
            VE.wait(hi_tok)
            if j >= 4:
                VE.wait(spl_state[("st", j - 4)])
            VE.op("tensor_sub", wlos[sl][:], w32s[sl][:], whis[sl][:], inc=s_ve)
            lo_tok = s_ve.tok
            SQ = GP if sl % 2 == 0 else SY
            SQ.wait(hi_tok)
            SQ.wait(lo_tok)
            for tgt_h, tgt_l, c0, c1 in writes:
                SQ.op("dma_start", out=tgt_h, in_=whis[sl][:, c0:c1],
                      inc=spl_st[sl], inc_k=16)
                SQ.op("dma_start", out=tgt_l, in_=wlos[sl][:, c0:c1],
                      inc=spl_st[sl], inc_k=16)
            spl_state[("st", j)] = spl_st[sl].tok
            split_done[0] = tuple(s.tok for s in spl_st)
            for ahead in range(1, 5):
                issue_split_load(j + ahead)

        wr_sems = [sems["Lw32"], sems["Lw32b"], sems["Lw32c"], sems["Lw32d"]]
        cp_tok = {}
        for u in range(2 * NDC):
            sl = u % 4
            dc, hh = divmod(u, 2)
            if u >= 4:
                SY.wait(s_se, cp_tok[u - 4])
            else:
                SY.wait(wr_sems[sl], wr_sems[sl].n)
            SY.op("dma_start", out=wr[sl][:],
                  in_=wfT_in[dc * 128:(dc + 1) * 128, hh * 512:(hh + 1) * 512],
                  inc=wr_sems[sl], inc_k=16)
            SE.wait(wr_sems[sl].tok)
            SE.op("copy", w8[:, u * 512:(u + 1) * 512], wr[sl][:], inc=s_se)
            cp_tok[u] = s_se.n
        w8_done = s_se.n
        SY.wait(s_se, w8_done)
        SY.op("dma_start", out=bias_p0[:], in_=bias_in, inc=s_w32, inc_k=16)
        PE.wait(s_w32.tok)
        PE.wait(s_se, w8_done)
        PE.wait(s_ve, ones_done)

        ck = {}

        def ffn_load(k):
            if k >= NT:
                return
            bi = k % 5
            if ("resid", k - 5) in ck:
                SY.wait(s_ve, ck[("resid", k - 5)])
            if ("y1f_of", k - 5) in ck:
                SY.wait(s_se, ck[("y1f_of", k - 5)])
            SY.op("dma_start", out=xt[bi][:], in_=src_hbm[k * 128:(k + 1) * 128, :],
                  inc=s_x[bi], inc_k=16)
            ck[("xld", k)] = s_x[bi].tok

        def ffn_lnpair(k):
            kA = k + 2 if (k + 2) < NT else None
            kB = (k - 2) if (ln2 and 0 <= (k - 2) < NT) else None
            if kA is not None:
                biA = kA % 5
                VE.wait(ck[("xld", kA)])
                VE.op("bn_stats", st(C_BN, 6), xt[biA][:, 0:512])
                VE.op("bn_stats", st(C_BN + 6, 6), xt[biA][:, 512:1024])
                VE.op("drain")
                VE.op("bn_aggr", st(C_MEAN, 2), st(C_BN, 12), inc=s_ve)
                ck[("aggrA", kA)] = s_ve.tok
            if kB is not None:
                biB = kB % 2
                VE.wait(s_ve, ck[("resid", kB)])
                VE.op("drain")
                VE.op("bn_stats", st(C_BN2, 6), ht[biB][:, 0:512])
                VE.op("bn_stats", st(C_BN2 + 6, 6), ht[biB][:, 512:1024])
                VE.op("drain")
                VE.op("bn_aggr", st(C_MEAN2, 2), st(C_BN2, 12), inc=s_ve)
                ck[("aggrB", kB)] = s_ve.tok
            if kA is not None:
                SE.wait(ck[("aggrA", kA)])
                SE.op("activation", st(C_SQ), st(C_VAR), AF.Sqrt, bias=eps_vec[:],
                      inc=s_se)
                ck[("sqA", kA)] = s_se.tok
            if kB is not None:
                SE.wait(ck[("aggrB", kB)])
                SE.op("activation", st(C_SQ2), st(C_VAR2), AF.Sqrt, bias=eps_vec[:],
                      inc=s_se)
                ck[("sqB", kB)] = s_se.tok
            if kA is not None:
                VE.wait(ck[("sqA", kA)])
                VE.op("reciprocal", st(C_RSTD), st(C_SQ))
                VE.op("drain")
                VE.op("scalar_tensor_tensor", st(C_NEGM), st(C_MEAN), -1.0,
                      st(C_RSTD), ALU.mult, ALU.mult, inc=s_ve)
                ck[("nmA", kA)] = s_ve.tok
            if kB is not None:
                VE.wait(ck[("sqB", kB)])
                VE.op("reciprocal", st(C_RSTD2), st(C_SQ2))
                VE.op("drain")
                VE.op("scalar_tensor_tensor", st(C_NEGM2), st(C_MEAN2), -1.0,
                      st(C_RSTD2), ALU.mult, ALU.mult, inc=s_ve)
                ck[("nmB", kB)] = s_ve.tok
            if kA is not None:
                SE.wait(ck[("nmA", kA)])
                SE.wait(ck[("xld", kA)])
                if ("tr", kA - 2) in ck:
                    SE.wait(ck[("tr", kA - 2)])
                SE.op("activation", y1f[kA % 2][:], xt[biA][:], AF.Identity,
                      bias=st(C_NEGM), scale=st(C_RSTD), inc=s_se)
                ck[("y1f", kA)] = s_se.n
                ck[("y1f_of", kA)] = s_se.n
            if kB is not None:
                SE.wait(ck[("nmB", kB)])
                if ("l2tr", kB - 1) in ck:
                    SE.wait(ck[("l2tr", kB - 1)])
                SE.op("activation", y2f[kB % 2][:], ht[biB][:], AF.Identity,
                      bias=st(C_NEGM2), scale=st(C_RSTD2), inc=s_se)
                ck[("y2f", kB)] = s_se.tok

        def ffn_tr(k):
            if k < 0 or k >= NT:
                return
            bi = k % 2
            pb = bi * 4
            PE.wait(s_se, ck[("y1f", k)])
            PE.wait(sems["Lident"], 16)
            if ("q8", k - 2) in ck:
                PE.wait(s_se, ck[("q8", k - 2)])
            if ln2 and ("l2lo", k - 2) in ck:
                PE.wait(ck[("l2lo", k - 2)])
            if ln2 and ("l2lo", k - 4) in ck:
                PE.wait(ck[("l2lo", k - 4)])
            for c in range(NDC):
                PE.op("transpose", ps[:, pb * 512 + c * 128: pb * 512 + (c + 1) * 128],
                      y1f[bi][:, c * 128:(c + 1) * 128], ident[:],
                      inc=(s_pe if c == NDC - 1 else None))
            ck[("tr", k)] = s_pe.tok
            SE.wait(ck[("tr", k)])
            if ("mm", k - 2) in ck:
                SE.wait(s_pe, ck[("mm", k - 2)])
            SE.op("activation", y8[bi][:], ps[:, pb * 512:(pb + 2) * 512], AF.Copy,
                  inc=s_se)
            ck[("q8", k)] = s_se.n

        def ffn_mm(k):
            if k < 0 or k >= NT:
                return
            bi = k % 2
            pb = bi * 4
            PE.wait(s_se, ck[("q8", k)])
            if ("f16", k - 2) in ck:
                PE.wait(s_se, ck[("f16", k - 2)])
            for dc in range(NDC):
                PE.op("matmul", bank(pb + 2), y8[bi][:, dc * 128:(dc + 1) * 128],
                      w8[:, dc * D: dc * D + 512], start=(dc == 0), stop=False)
                PE.op("matmul", bank(pb + 3), y8[bi][:, dc * 128:(dc + 1) * 128],
                      w8[:, dc * D + 512: (dc + 1) * D], start=(dc == 0), stop=False)
            PE.op("matmul", bank(pb + 2), ones_row[:], bias_p0[0:1, 0:512],
                  start=False, stop=True)
            PE.op("matmul", bank(pb + 3), ones_row[:], bias_p0[0:1, 512:1024],
                  start=False, stop=True, inc=s_pe)
            ck[("mm", k)] = s_pe.n
            SE.wait(s_pe, ck[("mm", k)])
            if ("t1", k - 2) in ck:
                SE.wait(s_ve, ck[("t1", k - 2)])
            SE.op("copy", f16t[k % 2][:], ps[:, (pb + 2) * 512:(pb + 4) * 512],
                  inc=s_se)
            ck[("f16", k)] = s_se.n

        def ffn_tail(k):
            if k < 0:
                return
            bi = k % 2
            VE.wait(s_se, ck[("f16", k)])
            if ("htst", k - 2) in ck:
                VE.wait(ck[("htst", k - 2)])
            if ("y2f", k - 2) in ck:
                VE.wait(ck[("y2f", k - 2)])
            VE.op("scalar_tensor_tensor", ht[bi][:], f16t[k % 2][:], 0.5, xt[k % 5][:],
                  ALU.mult, ALU.add, inc=s_ve)
            ck[("resid", k)] = s_ve.n
            ck[("t1", k)] = s_ve.n
            GP.wait(s_ve, ck[("resid", k)])
            GP.op("dma_start", out=dst_hbm[k * 128:(k + 1) * 128, :], in_=ht[bi][:],
                  inc=s_sh[bi], inc_k=16)
            ck[("htst", k)] = s_sh[bi].tok

        def ln2_tr(k):
            if not ln2 or k < 0 or k >= NT:
                return
            bi = k % 2
            pb = bi * 4
            PE.wait(ck[("y2f", k)])
            if ("l2lo", k - 2) in ck:
                PE.wait(ck[("l2lo", k - 2)])
            for c in range(NDC):
                PE.op("transpose", ps[:, pb * 512 + c * 128: pb * 512 + (c + 1) * 128],
                      y2f[bi][:, c * 128:(c + 1) * 128], ident[:],
                      inc=(s_pe if c == NDC - 1 else None))
            ck[("l2tr", k)] = s_pe.tok

        def ln2_evac(k):
            if not ln2 or k < 0 or k >= NT:
                return
            sq = k // NTS
            t = k % NTS
            bi = k % 2
            pb = bi * 4
            g4 = t % 4
            dst_h = y2sph[:].rearrange("p (c f) -> p c f", c=NDC)[:, :, g4 * 128:(g4 + 1) * 128]
            dst_l = y2spl[:].rearrange("p (c f) -> p c f", c=NDC)[:, :, g4 * 128:(g4 + 1) * 128]
            SE.wait(ck[("l2tr", k)])
            if ("l2st", k - (4 + g4)) in ck:
                SE.wait(ck[("l2st", k - (4 + g4))])
            SE.op("activation", dst_h, ps[:, pb * 512:(pb + 2) * 512],
                  AF.Copy, inc=s_se)
            hi_done = s_se.tok
            VE.wait(hi_done)
            VE.op("tensor_sub", dst_l, ps[:, pb * 512:(pb + 2) * 512],
                  dst_h, inc=s_ve)
            ck[("l2lo", k)] = s_ve.tok
            if g4 == 3:
                sp0 = (t - 3) // STRIPE
                GP.wait(ck[("l2lo", k)])
                for hs in range(4 // STRIPE):
                    src_h = y2sph[:].rearrange("p (c f) -> p c f", c=NDC)[
                        :, :, hs * SWc:(hs + 1) * SWc]
                    src_l = y2spl[:].rearrange("p (c f) -> p c f", c=NDC)[
                        :, :, hs * SWc:(hs + 1) * SWc]
                    GP.op("dma_start", out=y2h_s[sq, sp0 + hs], in_=src_h,
                          inc=s_sy[k % 2], inc_k=16)
                    GP.op("dma_start", out=y2l_s[sq, sp0 + hs], in_=src_l,
                          inc=s_sy[k % 2], inc_k=16)
                for kk in range(k - 3, k + 1):
                    ck[("l2st", kk)] = s_sy[k % 2].tok

        ffn_load(0)
        ffn_load(1)
        ffn_load(2)
        ffn_lnpair(-2)
        ffn_lnpair(-1)
        ffn_tr(0)
        for k in range(NT):
            ffn_tail(k - 2)
            ffn_load(k + 3)
            ffn_lnpair(k)
            ffn_mm(k)
            ffn_tr(k + 1)
            ln2_tr(k - 2)
            ln2_evac(k - 2)
            if splits:
                emit_split_unit()
                emit_split_unit()
                if k % 2 == 0:
                    emit_split_unit()
        for k in (NT - 2, NT - 1):
            ffn_tail(k)
            ffn_lnpair(k + 2)
            ln2_tr(k)
            ln2_evac(k)
        if splits:
            while spl_state["i"] < len(splits):
                emit_split_unit()
        barrier(skip_stores=("Sw", "Swb", "Swc", "Swd") if skip else ())

    def emit_copy_out(src_hbm):
        ar.phase_begin()
        cp = [ar.phase([128, D], F32) for _ in range(2)]
        s_x = [sems["Lx0"], sems["Lx1"]]
        s_sh = [sems["Sh0"], sems["Sh1"]]
        last_st = {}
        for t in range(NT):
            bi = t % 2
            if t >= 2:
                SY.wait(last_st[t - 2])
            SY.op("dma_start", out=cp[bi][:], in_=src_hbm[t * 128:(t + 1) * 128, :],
                  inc=s_x[bi], inc_k=16)
            GP.wait(s_x[bi].tok)
            GP.op("dma_start", out=out_ext[t * 128:(t + 1) * 128, :], in_=cp[bi][:],
                  inc=s_sh[bi], inc_k=16)
            last_st[t] = s_sh[bi].tok
        barrier()

    emit_ffn(x_in, h_s, ln2=True, splits=_split_units(), skip='Sw')
    if DEBUG_STOP == "ffn1":
        emit_copy_out(h_s)
        _finish(nc, stack, ALL)
        return nc

    for seq in range(SEQ_PER_CORE):
        base = seq * S

        attn_phase_begin()
        y2s2h = [ar.phase([128, NDC * 512], BF16) for _ in range(2)]
        y2s2l = [ar.phase([128, NDC * 512], BF16) for _ in range(2)]
        wvh_b = [ar.phase([128, D], BF16) for _ in range(4)]
        wvl_b = [ar.phase([128, D], BF16) for _ in range(4)]
        vh_e = [ar.phase([128, D], BF16) for _ in range(2)]
        vl_e = [ar.phase([128, D], BF16) for _ in range(2)]
        s_y2sp = sems["Ly2sp"]
        s_wv = [sems["Lwv0"], sems["Lwv1"], sems["Lwv2"], sems["Lwv3"]]
        s_sv = [sems["Sv0"], sems["Sv1"]]
        cons = {}

        def v_span_load(tb):
            if tb >= NTB or ("spl", tb) in cons:
                return
            if tb >= 2:
                SY.wait(s_pe, cons[("vmm", tb - 2)])
            for hs in range(512 // SWc):
                sp_i = tb * (512 // SWc) + hs
                dst_h = y2s2h[tb % 2][:].rearrange("p (c f) -> p c f", c=NDC)[
                    :, :, hs * SWc:(hs + 1) * SWc]
                dst_l = y2s2l[tb % 2][:].rearrange("p (c f) -> p c f", c=NDC)[
                    :, :, hs * SWc:(hs + 1) * SWc]
                SY.op("dma_start", out=dst_h, in_=y2h_s[seq, sp_i], inc=s_y2sp, inc_k=16)
                SY.op("dma_start", out=dst_l, in_=y2l_s[seq, sp_i], inc=s_y2sp, inc_k=16)
            cons[("spl", tb)] = s_y2sp.tok

        for tb in range(NTB):
            v_span_load(tb)
            span_ld = cons[("spl", tb)]
            y2sh, y2sl = y2s2h[tb % 2], y2s2l[tb % 2]
            v_span_load(tb + 1)
            for dc in range(NDC):
                wbi = dc % 4
                if dc >= 4:
                    SY.wait(s_pe, cons[("vdc", tb, dc - 4)])
                elif tb > 0:
                    SY.wait(s_pe, cons[("vdc", tb - 1, dc + 4)])
                if split_done[0] is not None:
                    for _t in split_done[0]:
                        SY.wait(_t)
                SY.op("dma_start", out=wvh_b[wbi][:], in_=wvh_s[dc], inc=s_wv[wbi],
                      inc_k=16)
                SY.op("dma_start", out=wvl_b[wbi][:], in_=wvl_s[dc], inc=s_wv[wbi],
                      inc_k=16)
                wv_ld = s_wv[wbi].tok
                PE.wait(span_ld)
                PE.wait(wv_ld)
                if dc == 0 and tb > 0:
                    PE.wait(s_ve, max(cons[("vlo", tb - 1, _tt)] for _tt in range(4)))
                for tt in range(4):
                    yh = y2sh[:, dc * 512 + tt * 128: dc * 512 + (tt + 1) * 128]
                    yl = y2sl[:, dc * 512 + tt * 128: dc * 512 + (tt + 1) * 128]
                    first, last = dc == 0, dc == NDC - 1
                    for eb in range(2):
                        PE.op("matmul", bank(tt * 2 + eb), yh,
                              wvh_b[wbi][:, eb * 512:(eb + 1) * 512], start=first,
                              stop=False)
                        PE.op("matmul", bank(tt * 2 + eb), yh,
                              wvl_b[wbi][:, eb * 512:(eb + 1) * 512], start=False,
                              stop=False)
                    for eb in range(2):
                        PE.op("matmul", bank(tt * 2 + eb), yl,
                              wvh_b[wbi][:, eb * 512:(eb + 1) * 512], start=False,
                              stop=last,
                              inc=(s_pe if (eb == 1 and tt == 3) else None))
                cons[("vdc", tb, dc)] = s_pe.n
            cons[("vmm", tb)] = v_mm = s_pe.n
            for tt in range(4):
                ebi = tt % 2
                SE.wait(s_pe, v_mm)
                if tt >= 2:
                    SE.wait(cons[("vst", tb, tt - 2)])
                    SE.wait(s_ve, cons[("vlo", tb, tt - 2)])
                SE.op("activation", vh_e[ebi][:], bank(tt * 2, 2), AF.Copy, inc=s_se)
                hi_d = s_se.n
                VE.wait(s_se, hi_d)
                VE.op("tensor_sub", vl_e[ebi][:], bank(tt * 2, 2), vh_e[ebi][:],
                      inc=s_ve)
                cons[("vlo", tb, tt)] = lo_d = s_ve.n
                GP.wait(s_ve, lo_d)
                GP.op("dma_start", out=vh_s[tb * 4 + tt], in_=vh_e[ebi][:],
                      inc=s_sv[ebi], inc_k=16)
                GP.op("dma_start", out=vl_s[tb * 4 + tt], in_=vl_e[ebi][:],
                      inc=s_sv[ebi], inc_k=16)
                cons[("vst", tb, tt)] = s_sv[ebi].tok
        barrier()

        attn_phase_begin()
        y2s2h = [ar.phase([128, NDC * 512], BF16) for _ in range(2)]
        y2s2l = [ar.phase([128, NDC * 512], BF16) for _ in range(2)]
        wkb_h = [ar.phase([128, NDC * 128], BF16) for _ in range(4)]
        wkb_l = [ar.phase([128, NDC * 128], BF16) for _ in range(4)]
        s_wk = [sems["Lwk0"], sems["Lwk1"], sems["Lwk2"], sems["Lwk3"]]
        cons = {}

        def k_span_load(tb):
            if tb >= NTB or ("spl", tb) in cons:
                return
            if tb >= 2:
                SY.wait(s_pe, cons[("kmm", tb - 2, 15)])
            for hs in range(512 // SWc):
                sp_i = tb * (512 // SWc) + hs
                dst_h = y2s2h[tb % 2][:].rearrange("p (c f) -> p c f", c=NDC)[
                    :, :, hs * SWc:(hs + 1) * SWc]
                dst_l = y2s2l[tb % 2][:].rearrange("p (c f) -> p c f", c=NDC)[
                    :, :, hs * SWc:(hs + 1) * SWc]
                SY.op("dma_start", out=dst_h, in_=y2h_s[seq, sp_i], inc=s_y2sp, inc_k=16)
                SY.op("dma_start", out=dst_l, in_=y2l_s[seq, sp_i], inc=s_y2sp, inc_k=16)
            cons[("spl", tb)] = s_y2sp.tok

        for tb in range(NTB):
            k_span_load(tb)
            span_ld = cons[("spl", tb)]
            y2sh, y2sl = y2s2h[tb % 2], y2s2l[tb % 2]
            k_span_load(tb + 1)
            for et in range(16):
                wbi = et % 4
                if et >= 4:
                    SY.wait(s_pe, cons[("kmm", tb, et - 4)])
                elif tb > 0:
                    SY.wait(s_pe, cons[("kmm", tb - 1, et + 12)])
                SY.op("dma_start", out=wkb_h[wbi][:], in_=wkh_s[et], inc=s_wk[wbi],
                      inc_k=16)
                SY.op("dma_start", out=wkb_l[wbi][:], in_=wkl_s[et], inc=s_wk[wbi],
                      inc_k=16)
                wk_ld = s_wk[wbi].tok
                pbk = et % 2
                PE.wait(span_ld)
                PE.wait(wk_ld)
                prev_lo = cons.get(("klo", tb, et - 2)) or \
                    (cons.get(("klo", tb - 1, 14 + et)) if et < 2 else None)
                if prev_lo is not None:
                    PE.wait(prev_lo)
                for dc in range(NDC):
                    wh = wkb_h[wbi][:, dc * 128:(dc + 1) * 128]
                    wl = wkb_l[wbi][:, dc * 128:(dc + 1) * 128]
                    first, last = dc == 0, dc == NDC - 1
                    PE.op("matmul", bank(pbk), wh, y2sh[:, dc * 512:(dc + 1) * 512],
                          start=first, stop=False)
                    PE.op("matmul", bank(pbk), wh, y2sl[:, dc * 512:(dc + 1) * 512],
                          start=False, stop=False)
                    PE.op("matmul", bank(pbk), wl, y2sh[:, dc * 512:(dc + 1) * 512],
                          start=False, stop=last, inc=(s_pe if last else None))
                cons[("kmm", tb, et)] = kmm = s_pe.n
                m = 1 if et < 8 else 2
                col0 = (et % 8) * S + tb * 512
                SE.wait(s_pe, kmm)
                SE.op("activation", k_sb[(m, "h")][:, col0:col0 + 512], bank(pbk),
                      AF.Copy, inc=s_se)
                khi = s_se.n
                VE.wait(s_se, khi)
                VE.op("tensor_sub", k_sb[(m, "l")][:, col0:col0 + 512], bank(pbk),
                      k_sb[(m, "h")][:, col0:col0 + 512], inc=s_ve)
                cons[("klo", tb, et)] = s_ve.tok
        barrier()

        attn_phase_begin()
        SW = 128 * STRIPE
        y2t_mark = ar.mark()
        y2th = ar.phase([128, NDC * SW], BF16)
        y2tl = ar.phase([128, NDC * SW], BF16)
        wqp = [ar.phase([128, D], BF16) for _ in range(6)]
        qtm_e1 = ar.phase([128, max(2 * D, S)], F32)
        qh = ar.phase([128, STRIPE * 2 * D], BF16)
        ql = ar.phase([128, STRIPE * 2 * D], BF16)
        dpp_mark = ar.mark()
        dpp_full = ar.phase([128, max(S, 2 * D)], F32)
        dpp = dpp_full
        dth = ar.at(y2t_mark, [128, S], BF16)
        dtl = ar.at(y2t_mark + S * 2, [128, S], BF16)
        att = ar.phase([128, D], F32)
        vst_h = [ar.phase([128, D], BF16) for _ in range(3)]
        vst_l = [ar.phase([128, D], BF16) for _ in range(3)]
        h2t = ar.phase([128, D], F32)
        q_tm_bufs = [qtm_e1, dpp_full]
        e1 = qtm_e1
        s_y2t, s_ht, s_sh2 = sems["Ly2t"], sems["Lht"], sems["Sh2"]
        s_wq = [sems["Lwq%d" % i] for i in range(6)]
        wq_hist = []
        s_v = [sems["Lv0"], sems["Lv1"], sems["Lv2"]]

        n_stripes = NTS // STRIPE
        tk = {}

        def SUM1(i):
            return st(C_SUM1A + (i % 2))

        def NINV1(i):
            return st(C_NINV1A + (i % 2))

        def emit_scores(i, mat):
            tt = i % STRIPE
            b0 = (mat - 1) * 4
            if mat == 1:
                PE.wait(tk[("qse", i)])
                PE.wait(tk[("qve", i)])
                if ("exp1", i - 1) in tk:
                    PE.wait(tk[("exp1", i - 1)])
            else:
                g_last = i - (i % STRIPE) + STRIPE - 1
                PE.wait(tk[("qse", g_last)])
                PE.wait(tk[("qve", g_last)])
                if ("dtev", i - 1) in tk:
                    PE.wait(tk[("dtes", i - 1)])
                    PE.wait(tk[("dtev", i - 1)])
                if ("attev", i - 1) in tk:
                    PE.wait(tk[("attev", i - 1)])
            kh = k_sb[(mat, "h")]
            kl = k_sb[(mat, "l")]
            for tb in range(NTB):
                for ec in range(NDC):
                    gec = (mat - 1) * 8 + ec
                    qh_sl = qh[:, tt * 2048 + gec * 128: tt * 2048 + (gec + 1) * 128]
                    ql_sl = ql[:, tt * 2048 + gec * 128: tt * 2048 + (gec + 1) * 128]
                    first, last = ec == 0, ec == NDC - 1
                    ksl = slice(ec * S + tb * 512, ec * S + (tb + 1) * 512)
                    PE.op("matmul", bank(b0 + tb), qh_sl, kh[:, ksl],
                          start=first, stop=False)
                    PE.op("matmul", bank(b0 + tb), qh_sl, kl[:, ksl],
                          start=False, stop=False)
                    PE.op("matmul", bank(b0 + tb), ql_sl, kh[:, ksl],
                          start=False, stop=last,
                          inc=(s_pe if last else None))
                tk[("s%d_b%d" % (mat, tb), i)] = s_pe.tok
            tk[("s%d" % mat, i)] = s_pe.tok

        def emit_max(i, mat):
            b0 = (mat - 1) * 4
            VE.op("drain")
            for tb in range(NTB):
                VE.wait(tk[("s%d_b%d" % (mat, tb), i)])
                VE.op("reduce_max", st(C_MB + tb), bank(b0 + tb), axis=AX.X)
            VE.op("drain")
            nms = C_NMS1 if mat == 1 else C_NMS2
            if NTB == 4:
                VE.op("tensor_max", st(C_MA), st(C_MB), st(C_MB + 1))
                VE.op("tensor_max", st(C_MBB), st(C_MB + 2), st(C_MB + 3))
                VE.op("drain")
                VE.op("tensor_scalar", st(nms), st(C_MA), st(C_MBB), -SCALE,
                      ALU.max, ALU.mult, inc=s_ve)
            else:
                VE.op("tensor_scalar_mul", st(nms), st(C_MB), -SCALE, inc=s_ve)
            tk[("nms%d" % mat, i)] = s_ve.tok

        def emit_exp1(i):
            SE.wait(tk[("s1", i)])
            SE.wait(tk[("nms1", i)])
            SE.op("activation", e1[:, 0:S], ps[:, 0:S], AF.Exp,
                  bias=st(C_NMS1), scale=SCALE, accum_out=SUM1(i), inc=s_se)
            tk[("exp1", i)] = s_se.tok

        def emit_exp2_dpp(i):
            SE.wait(tk[("s2", i)])
            SE.wait(tk[("nms2", i)])
            SE.op("activation", ps[:, 2048:2048 + S], ps[:, 2048:2048 + S], AF.Exp,
                  bias=st(C_NMS2), scale=SCALE, accum_out=st(C_SUM2), inc=s_se)
            tk[("exp2", i)] = s_se.tok
            VE.wait(tk[("exp1", i)])
            VE.op("reciprocal", NINV1(i), SUM1(i))
            VE.op("drain")
            VE.op("tensor_scalar_mul", NINV1(i), NINV1(i), -1.0)
            VE.wait(tk[("exp2", i)])
            VE.op("reciprocal", st(C_R2), st(C_SUM2))
            VE.op("drain")
            VE.op("scalar_tensor_tensor", st(C_C), SUM1(i), lam_vec[:], st(C_R2),
                  ALU.mult, ALU.mult)
            VE.op("drain")
            HS_ = S // 2
            VE.op("scalar_tensor_tensor", dpp[:, 0:HS_], ps[:, 2048:2048 + HS_],
                  st(C_C), e1[:, 0:HS_], ALU.mult, ALU.subtract, inc=s_ve)
            tk[("dpp0", i)] = s_ve.tok
            VE.op("scalar_tensor_tensor", dpp[:, HS_:S], ps[:, 2048 + HS_:2048 + S],
                  st(C_C), e1[:, HS_:S], ALU.mult, ALU.subtract, inc=s_ve)
            tk[("dpp", i)] = s_ve.tok

        def emit_trd_evac(i):
            PE.wait(tk[("dpp0", i)])
            half = NTS // 2
            for c in range(NTS):
                if c == half:
                    PE.wait(tk[("dpp", i)])
                PE.op("transpose", ps[:, 2048 + c * 128: 2048 + (c + 1) * 128],
                      dpp[:, c * 128:(c + 1) * 128], ident[:],
                      inc=(s_pe if c in (half - 1, NTS - 1) else None))
                if c == half - 1:
                    tk[("trd0", i)] = s_pe.tok
            tk[("trd1", i)] = s_pe.tok
            HS = S // 2
            SE.wait(tk[("trd0", i)])
            SE.op("activation", dth[:, 0:HS], ps[:, 2048:2048 + HS], AF.Copy, inc=s_se)
            tk[("dth0", i)] = s_se.tok
            VE.wait(tk[("dth0", i)])
            VE.op("tensor_sub", dtl[:, 0:HS], ps[:, 2048:2048 + HS], dth[:, 0:HS], inc=s_ve)
            tk[("dtl0", i)] = s_ve.tok
            SE.wait(tk[("trd1", i)])
            SE.op("activation", dth[:, HS:S], ps[:, 2048 + HS:2048 + S], AF.Copy, inc=s_se)
            tk[("dtes", i)] = s_se.tok
            VE.wait(tk[("dtes", i)])
            VE.op("tensor_sub", dtl[:, HS:S], ps[:, 2048 + HS:2048 + S], dth[:, HS:S],
                  inc=s_ve)
            tk[("dtev", i)] = s_ve.tok

        def emit_pv(i, seq_base):
            PE.wait(tk[("dth0", i)])
            PE.wait(tk[("dtl0", i)])
            if S < 2048:
                PE.wait(tk[("dtes", i)])
                PE.wait(tk[("dtev", i)])
            for tc in range(NTS):
                vbi = tc % 3
                if tc >= 3:
                    prev = tk.get(("pvc", i, tc - 3))
                else:
                    prev_c = max(c for c in range(NTS) if c % 3 == vbi)
                    prev = tk.get(("pvc", i - 1, prev_c))
                if prev is not None:
                    SY.wait(prev)
                SY.op("dma_start", out=vst_h[vbi][:], in_=vh_s[tc], inc=s_v[vbi],
                      inc_k=16)
                SY.op("dma_start", out=vst_l[vbi][:], in_=vl_s[tc], inc=s_v[vbi],
                      inc_k=16)
                PE.wait(s_v[vbi].tok)
                if tc == NTS // 2:
                    PE.wait(tk[("dtes", i)])
                    PE.wait(tk[("dtev", i)])
                first, last = tc == 0, tc == NTS - 1
                for eb in range(2):
                    PE.op("matmul", bank(4 + eb), dth[:, tc * 128:(tc + 1) * 128],
                          vst_h[vbi][:, eb * 512:(eb + 1) * 512], start=first, stop=False)
                    PE.op("matmul", bank(4 + eb), dth[:, tc * 128:(tc + 1) * 128],
                          vst_l[vbi][:, eb * 512:(eb + 1) * 512], start=False, stop=False)
                for eb in range(2):
                    PE.op("matmul", bank(4 + eb), dtl[:, tc * 128:(tc + 1) * 128],
                          vst_h[vbi][:, eb * 512:(eb + 1) * 512], start=False,
                          stop=last, inc=(s_pe if eb == 1 else None))
                tk[("pvc", i, tc)] = s_pe.tok
            tk[("pv", i)] = s_pe.tok

        def emit_epilogueA(i, seq_base):
            SE.wait(tk[("pv", i)])
            if ("h2", i - 1) in tk:
                SE.wait(tk[("h2", i - 1)])
            if ("h2st", i - 1) in tk:
                SE.wait(tk[("h2st", i - 1)])
            SE.op("activation", att[:], bank(4, 2), AF.Copy, scale=NINV1(i))
            SE.op("activation", h2t[:], bank(4, 2), AF.Square, scale=NINV1(i),
                  accum_out=st(C_SSQ), inc=s_se)
            tk[("attev", i)] = s_se.tok

        def emit_epilogueB(i, seq_base):
            if i < seq * NTS or i >= (seq + 1) * NTS:
                return
            qt = i % NTS
            if ("h2st", i - 1) in tk:
                SY.wait(tk[("h2st", i - 1)])
            SY.wait(tk[("attev", i)])
            SY.op("dma_start", out=h2t[:],
                  in_=h_s[seq_base + qt * 128: seq_base + qt * 128 + 128, :],
                  inc=s_ht, inc_k=16)
            ht_ld = s_ht.tok
            SE.op("drain")
            SE.op("activation", st(C_RMS), st(C_SSQ), AF.Sqrt, bias=eps_vec[:],
                  scale=float(1.0 / D), inc=s_se)
            rms_done = s_se.tok
            VE.wait(rms_done)
            VE.op("reciprocal", st(C_FS), st(C_RMS))
            VE.op("drain")
            VE.op("tensor_scalar_mul", st(C_FS), st(C_FS), float(OUT_MUL))
            VE.op("drain")
            VE.wait(ht_ld)
            VE.op("scalar_tensor_tensor", h2t[:], att[:], st(C_FS), h2t[:],
                  ALU.mult, ALU.add, inc=s_ve)
            tk[("h2", i)] = s_ve.tok
            GP.wait(tk[("h2", i)])
            GP.op("dma_start", out=h2_s[seq_base + qt * 128: seq_base + qt * 128 + 128, :],
                  in_=h2t[:], inc=s_sh2, inc_k=16)
            tk[("h2st", i)] = s_sh2.tok

        for sp in range(n_stripes):
            t0 = sp * STRIPE
            ga = seq * NTS + t0
            gb = ga + 1
            if ("pv", ga - 1) in tk:
                SE.wait(tk[("pv", ga - 1)])
            SE.op("dma_start", out=y2th[:], in_=y2h_s[seq, t0 // STRIPE],
                  inc=s_y2t, inc_k=16)
            SE.op("dma_start", out=y2tl[:], in_=y2l_s[seq, t0 // STRIPE],
                  inc=s_y2t, inc_k=16)
            y2t_ld = s_y2t.tok
            if ("exp1", ga - 1) in tk:
                PE.wait(tk[("exp1", ga - 1)])
            if ("attev", ga - 1) in tk:
                PE.wait(tk[("attev", ga - 1)])
            if ("dtev", ga - 1) in tk:
                PE.wait(tk[("dtes", ga - 1)])
                PE.wait(tk[("dtev", ga - 1)])
            for dc in range(NDC):
                halves = [(wqh_s[dc, :, 0:D], "h0", (dc % 2) * 2),
                          (wqh_s[dc, :, D:2 * D], "h1", (dc % 2) * 2 + 1),
                          (wql_s[dc, :, 0:D], "l0", 4),
                          (wql_s[dc, :, D:2 * D], "l1", 5)]
                sl = {}
                for src_ap_, nm_, j in halves:
                    depth = 2 if nm_[0] == "h" else 1
                    if len(wq_hist) >= depth:
                        SE.wait(wq_hist[-depth])
                    SE.op("dma_start", out=wqp[j][:], in_=src_ap_, inc=s_wq[j],
                          inc_k=16)
                    sl[nm_] = (j, s_wq[j].tok)
                PE.wait(y2t_ld)
                first, last = dc == 0, dc == NDC - 1
                for tt in range(STRIPE):
                    yh = y2th[:, dc * SW + tt * 128: dc * SW + (tt + 1) * 128]
                    for eb in range(4):
                        hs, htok = sl["h0"] if eb < 2 else sl["h1"]
                        PE.wait(htok)
                        PE.op("matmul", bank(tt * 4 + eb), yh,
                              wqp[hs][:, (eb % 2) * 512:(eb % 2 + 1) * 512],
                              start=first, stop=False)
                for tt in range(STRIPE):
                    yl = y2tl[:, dc * SW + tt * 128: dc * SW + (tt + 1) * 128]
                    for eb in range(4):
                        hs, htok = sl["h0"] if eb < 2 else sl["h1"]
                        PE.op("matmul", bank(tt * 4 + eb), yl,
                              wqp[hs][:, (eb % 2) * 512:(eb % 2 + 1) * 512],
                              start=False, stop=False)
                for tt in range(STRIPE):
                    yh = y2th[:, dc * SW + tt * 128: dc * SW + (tt + 1) * 128]
                    for eb in range(4):
                        ls, ltok = sl["l0"] if eb < 2 else sl["l1"]
                        PE.wait(ltok)
                        PE.op("matmul", bank(tt * 4 + eb), yh,
                              wqp[ls][:, (eb % 2) * 512:(eb % 2 + 1) * 512],
                              start=False, stop=last,
                              inc=(s_pe if (eb == 3 and tt == STRIPE - 1) else None))
                wq_hist.append(s_pe.tok)
            tk[("qpmm", seq * 100 + sp)] = s_pe.tok
            tk[("qpmm_last",)] = s_pe.tok
            qp_mm = s_pe.tok
            for tt in range(STRIPE):
                g = ga + tt
                q_tm = q_tm_bufs[tt % 2]
                VE.wait(qp_mm)
                VE.op("tensor_copy", q_tm[:, 0:2 * D], bank(tt * 4, 4), inc=s_ve)
                qtm_done = s_ve.tok
                PE.wait(qtm_done)
                for c in range(16):
                    PE.op("transpose",
                          ps[:, tt * 4 * 512 + c * 128: tt * 4 * 512 + (c + 1) * 128],
                          q_tm[:, c * 128:(c + 1) * 128], ident[:],
                          inc=(s_pe if c == 15 else None))
                tk[("qtr", g)] = s_pe.tok
                SE.wait(tk[("qtr", g)])
                SE.op("activation", qh[:, tt * 2048:(tt + 1) * 2048], bank(tt * 4, 4),
                      AF.Copy, inc=s_se)
                tk[("qse", g)] = s_se.tok
                VE.wait(tk[("qse", g)])
                VE.op("tensor_sub", ql[:, tt * 2048:(tt + 1) * 2048], bank(tt * 4, 4),
                      qh[:, tt * 2048:(tt + 1) * 2048], inc=s_ve)
                tk[("qve", g)] = s_ve.tok

            emit_scores(ga, 1)
            emit_max(ga, 1)
            emit_exp1(ga)
            emit_scores(ga, 2)
            emit_max(ga, 2)
            emit_exp2_dpp(ga)
            emit_scores(gb, 1)
            emit_max(gb, 1)
            emit_epilogueB(ga - 1, base)
            emit_trd_evac(ga)
            emit_exp1(gb)
            emit_pv(ga, base)
            emit_epilogueA(ga, base)
            emit_scores(gb, 2)
            emit_max(gb, 2)
            emit_exp2_dpp(gb)
            emit_epilogueB(ga, base)
            emit_trd_evac(gb)
            emit_pv(gb, base)
            emit_epilogueA(gb, base)
        emit_epilogueB(seq * NTS + NTS - 1, base)
        barrier()

    if DEBUG_STOP == "attn":
        emit_copy_out(h2_s)
        _finish(nc, stack, ALL)
        return nc

    emit_ffn(h2_s, out_ext)

    _finish(nc, stack, ALL)
    return nc


def _finish(nc, stack, ALL):
    blk = stack.enter_context(nc.Block())
    blk.sync(ALL["sync"].replay)
    blk.tensor(ALL["tensor"].replay)
    blk.vector(ALL["vector"].replay)
    blk.scalar(ALL["scalar"].replay)
    blk.gpsimd(ALL["gpsimd"].replay)
    stack.close()


def kernel(L0_attn_x, L0_attn_w_q12, L0_attn_w_k12, L0_attn_w_v,
           L0_attn_lambda_q1, L0_attn_lambda_k1, L0_attn_lambda_q2,
           L0_attn_lambda_k2, L0_ffn_weight, L0_ffn_bias):
    x = np.ascontiguousarray(np.asarray(L0_attn_x, dtype=np.float32))
    wq = np.ascontiguousarray(np.asarray(L0_attn_w_q12, dtype=np.float32))
    wk = np.ascontiguousarray(np.asarray(L0_attn_w_k12, dtype=np.float32))
    wv = np.ascontiguousarray(np.asarray(L0_attn_w_v, dtype=np.float32))
    wfT = np.ascontiguousarray(np.asarray(L0_ffn_weight, dtype=np.float32).T)
    bias = np.ascontiguousarray(np.asarray(L0_ffn_bias, dtype=np.float32).reshape(1, D))
    lams = [np.ascontiguousarray(np.asarray(v, dtype=np.float32).reshape(128, 8))
            for v in (L0_attn_lambda_q1, L0_attn_lambda_k1,
                      L0_attn_lambda_q2, L0_attn_lambda_k2)]

    nc = build_nc()
    in_maps = []
    for c in range(NCORES):
        xc = np.ascontiguousarray(
            x[c * SEQ_PER_CORE:(c + 1) * SEQ_PER_CORE].reshape(T, D))
        in_maps.append({"x": xc, "wq": wq, "wk": wk, "wv": wv, "wfT": wfT,
                        "bias": bias, "lq1": lams[0], "lk1": lams[1],
                        "lq2": lams[2], "lk2": lams[3]})
    res = run_bass_kernel_spmd(nc, in_maps, core_ids=list(range(NCORES)))
    out = np.empty((B, S, D), dtype=np.float32)
    for c in range(NCORES):
        out[c * SEQ_PER_CORE:(c + 1) * SEQ_PER_CORE] = \
            res.results[c]["out"].reshape(SEQ_PER_CORE, S, D)
    return out


# version 37
# speedup vs baseline: 1.0773x; 1.0004x over previous
import numpy as np
import concourse.bass as bass
import concourse.mybir as mybir
from concourse.bass_utils import run_bass_kernel_spmd

F32 = mybir.dt.float32
BF16 = mybir.dt.bfloat16
F16 = mybir.dt.float16
F8 = mybir.dt.float8e4
U8 = mybir.dt.uint8
AF = mybir.ActivationFunctionType
ALU = mybir.AluOpType
AX = mybir.AxisListType

D = 1024
NDC = D // 128
SCALE = float(D ** (-0.25))
LAMBDA_INIT = 0.8
EPS = 1e-5
OUT_MUL = 1.0 - LAMBDA_INIT
STRIPE = 2
DEBUG_STOP = None

B = S = NCORES = SEQ_PER_CORE = T = NT = NTS = NTB = None


def _set_dims(b, s, ncores):
    global B, S, NCORES, SEQ_PER_CORE, T, NT, NTS, NTB
    B, S, NCORES = b, s, ncores
    SEQ_PER_CORE = B // NCORES
    T = SEQ_PER_CORE * S
    NT = T // 128
    NTS = S // 128
    NTB = S // 512
    assert S % 512 == 0 and NTS % STRIPE == 0


_set_dims(16, 2048, 8)


class Sem:
    def __init__(self, handle, name):
        self.h = handle
        self.name = name
        self.n = 0

    def bump(self, k=1):
        self.n += k
        return self.n

    @property
    def tok(self):
        return (self, self.n)


class ES:
    def __init__(self, name):
        self.name = name
        self.ops = []
        self._waited = {}

    def wait(self, sem, val=None):
        if isinstance(sem, tuple):
            sem, val = sem
        if val is None:
            val = sem.n
        if val <= 0:
            return
        if self._waited.get(sem.name, -1) >= val:
            return
        self._waited[sem.name] = val
        self.ops.append(("w", sem, val))

    def op(self, meth, *args, inc=None, inc_k=1, **kw):
        if inc is not None:
            inc.bump(inc_k)
            self.ops.append(("o", meth, args, kw, inc, inc_k))
            return inc.n
        self.ops.append(("o", meth, args, kw, None, 0))
        return None

    def replay(self, eng):
        for item in self.ops:
            if item[0] == "w":
                eng.wait_ge(item[1].h, item[2])
            else:
                _, meth, args, kw, inc, k = item
                instr = getattr(eng, meth)(*args, **kw)
                if inc is not None:
                    instr.then_inc(inc.h, k)


class Arena:
    def __init__(self, nc, total_bytes):
        self.nc = nc
        self.guard = nc.sbuf_tensor("arena", [128, total_bytes], U8)
        self.h = self.guard.__enter__()
        self.base = nc.lookup_mloc(self.h).addr
        self.per_off = 0
        self.ph_off = None
        self.total = total_bytes
        self.uid = 0

    def _alloc(self, shape, dtype, off):
        per_part = int(np.prod(shape[1:])) * mybir.dt.size(dtype)
        off_al = (off + 31) // 32 * 32
        name = f"ar{self.uid}"
        self.uid += 1
        hnd = self.nc.virtual_sbuf_tensor(name, shape, dtype)
        self.nc._tensor(hnd.name, list(shape), dtype, type="SB")
        mloc = self.nc.lookup_mloc(hnd)
        mloc.allocated = True
        mloc.addr = self.base + off_al
        return hnd, off_al + per_part

    def persist(self, shape, dtype):
        h, end = self._alloc(shape, dtype, self.per_off)
        self.per_off = end
        assert self.per_off <= self.total, f"persist overflow {self.per_off}"
        return h

    def phase_begin(self):
        self.ph_off = self.per_off

    def mark(self):
        return self.ph_off

    def at(self, off, shape, dtype):
        h, end = self._alloc(shape, dtype, off)
        assert end <= self.total
        return h

    def phase(self, shape, dtype):
        h, end = self._alloc(shape, dtype, self.ph_off)
        self.ph_off = end
        assert self.ph_off <= self.total, \
            f"phase overflow {self.ph_off} > {self.total}"
        return h


def build_nc():
    nc = bass.Bass("TRN2", target_bir_lowering=False)

    x_in = nc.dram_tensor("x", [T, D], F32, kind="ExternalInput").ap()
    wq_in = nc.dram_tensor("wq", [D, 2 * D], F32, kind="ExternalInput").ap()
    wk_in = nc.dram_tensor("wk", [D, 2 * D], F32, kind="ExternalInput").ap()
    wv_in = nc.dram_tensor("wv", [D, D], F32, kind="ExternalInput").ap()
    wfT_in = nc.dram_tensor("wfT", [D, D], F32, kind="ExternalInput").ap()
    bias_in = nc.dram_tensor("bias", [1, D], F32, kind="ExternalInput").ap()
    lq1_in = nc.dram_tensor("lq1", [128, 8], F32, kind="ExternalInput").ap()
    lk1_in = nc.dram_tensor("lk1", [128, 8], F32, kind="ExternalInput").ap()
    lq2_in = nc.dram_tensor("lq2", [128, 8], F32, kind="ExternalInput").ap()
    lk2_in = nc.dram_tensor("lk2", [128, 8], F32, kind="ExternalInput").ap()
    out_ext = nc.dram_tensor("out", [T, D], F32, kind="ExternalOutput").ap()

    h_s = nc.dram_tensor("h_s", [T, D], F32, kind="Internal").ap()
    h2_s = nc.dram_tensor("h2_s", [T, D], F32, kind="Internal").ap()
    SWc = 128 * STRIPE
    NSP = S // SWc
    y2h_s = nc.dram_tensor("y2h_s", [SEQ_PER_CORE, NSP, 128, NDC * SWc], BF16, kind="Internal").ap()
    y2l_s = nc.dram_tensor("y2l_s", [SEQ_PER_CORE, NSP, 128, NDC * SWc], BF16, kind="Internal").ap()
    vh_s = nc.dram_tensor("vh_s", [NTS, 128, D], BF16, kind="Internal").ap()
    vl_s = nc.dram_tensor("vl_s", [NTS, 128, D], BF16, kind="Internal").ap()
    wqh_s = nc.dram_tensor("wqh_s", [NDC, 128, 2 * D], BF16, kind="Internal").ap()
    wql_s = nc.dram_tensor("wql_s", [NDC, 128, 2 * D], BF16, kind="Internal").ap()
    wkh_s = nc.dram_tensor("wkh_s", [16, 128, NDC * 128], BF16, kind="Internal").ap()
    wkl_s = nc.dram_tensor("wkl_s", [16, 128, NDC * 128], BF16, kind="Internal").ap()
    wvh_s = nc.dram_tensor("wvh_s", [NDC, 128, D], BF16, kind="Internal").ap()
    wvl_s = nc.dram_tensor("wvl_s", [NDC, 128, D], BF16, kind="Internal").ap()

    import contextlib
    stack = contextlib.ExitStack()
    E = stack.enter_context

    ps = E(nc.psum_tensor("ps", [128, 4096], F32))

    def bank(b, n=1):
        return ps[:, b * 512:(b + n) * 512]

    sem_names = ["pe", "ve", "se",
                 "b_sy", "b_pe", "b_ve", "b_se", "b_gp",
                 "Lident", "Lli", "Lw32", "Lx0", "Lx1", "Ly2sp", "Lwv0", "Lwv1", "Lwv2", "Lwv3",
                 "Lwk0", "Lwk1", "Lwk2", "Lwk3", "Ly2t", "Lwq0", "Lwq1", "Lwq2", "Lwq3", "Lwq4", "Lwq5", "Lv0", "Lv1", "Lv2", "Lht", "Lw32b", "Lw32c", "Lw32d", "Lx2", "Lx3",
                 "Sw", "Swb", "Swc", "Swd", "Sh0", "Sh1", "Sy0", "Sy1", "Sv0", "Sv1", "Sh2"]
    sems = {nm: Sem(E(nc.semaphore("sem_" + nm)), nm) for nm in sem_names}
    s_pe, s_ve, s_se = sems["pe"], sems["ve"], sems["se"]
    store_sems = [sems[nm] for nm in
                  ["Sw", "Swb", "Swc", "Swd", "Sh0", "Sh1", "Sy0", "Sy1", "Sv0",
                   "Sv1", "Sh2"]]

    SY, PE, VE, SE, GP = ES("sync"), ES("tensor"), ES("vector"), ES("scalar"), ES("gpsimd")
    ALL = {"sync": SY, "tensor": PE, "vector": VE, "scalar": SE, "gpsimd": GP}
    bar_sem = {"sync": sems["b_sy"], "tensor": sems["b_pe"], "vector": sems["b_ve"],
               "scalar": sems["b_se"], "gpsimd": sems["b_gp"]}
    bar_count = [0]

    def barrier(skip_stores=()):
        bar_count[0] += 1
        k = bar_count[0]
        for ssem in store_sems:
            if ssem.name not in skip_stores:
                GP.wait(ssem, ssem.n)
        for nm, es in ALL.items():
            es.op("drain", inc=bar_sem[nm])
        for nm, es in ALL.items():
            for nm2, sem2 in bar_sem.items():
                if nm2 != nm:
                    es.wait(sem2, k)

    AR_BYTES = 212832
    ar = Arena(nc, AR_BYTES)

    ident = ar.persist([128, 128], F32)
    ones_col = ar.persist([128, 1], F32)
    ones_row = ar.persist([1, 128], F32)
    lam_vec = ar.persist([128, 1], F32)
    stat = ar.persist([128, 64], F32)
    sml = ar.persist([1, 16], F32)
    eps_vec = ar.persist([128, 1], F32)
    bias_p0 = ar.persist([1, D], F32)

    attn_base = (ar.per_off + 31) // 32 * 32
    k_sb = {}
    _o = attn_base
    for m in (1, 2):
        for p in ("h", "l"):
            k_sb[(m, p)] = ar.at(_o, [128, NDC * S], BF16)
            _o += NDC * S * 2
    ATTN_PH = _o

    def attn_phase_begin():
        ar.ph_off = ATTN_PH

    ident_h = nc.inline_tensor(np.eye(128, dtype=np.float32), name="identity_c")

    (C_BN, C_MEAN, C_VAR, C_SQ, C_RSTD, C_NEGM,
     C_MB, C_MA, C_MBB, C_NMS1, C_NMS2, C_SUM1A, C_SUM1B, C_SUM2,
     C_R2, C_C, C_NINV1A, C_NINV1B, C_SSQ, C_RMS, C_FS) = \
        (0, 12, 13, 14, 15, 16, 17, 21, 22, 23, 24, 25, 26, 27, 28, 29, 30, 31, 32, 33, 34)
    (C_BN2, C_MEAN2, C_VAR2, C_SQ2, C_RSTD2, C_NEGM2) = (36, 48, 49, 50, 51, 52)

    def st(c, n=1):
        return stat[:, c:c + n]

    ar.phase_begin()
    lam_bufs = [ar.phase([128, 8], F32) for _ in range(4)]

    s_li, s_w32, s_sw = sems["Lli"], sems["Lw32"], sems["Sw"]
    SY.op("dma_start", out=ident[:], in_=ident_h.ap(), inc=sems["Lident"], inc_k=16)
    for i, tin in enumerate([lq1_in, lk1_in, lq2_in, lk2_in]):
        SY.op("dma_start", out=lam_bufs[i][:], in_=tin, inc=s_li, inc_k=16)
    lam_ld = s_li.tok
    VE.op("memset", ones_col[:], 1.0)
    VE.op("memset", ones_row[:], 1.0)
    VE.op("memset", eps_vec[:], float(EPS))
    VE.op("drain", inc=s_ve)
    ones_done = s_ve.n
    VE.wait(lam_ld)
    VE.op("tensor_mul", lam_bufs[0][:], lam_bufs[0][:], lam_bufs[1][:])
    VE.op("tensor_mul", lam_bufs[2][:], lam_bufs[2][:], lam_bufs[3][:])
    VE.op("drain")
    VE.op("reduce_sum", lam_bufs[1][:, 0:1], lam_bufs[0][:], axis=AX.X)
    VE.op("reduce_sum", lam_bufs[3][:, 0:1], lam_bufs[2][:], axis=AX.X)
    VE.op("drain", inc=s_ve)
    lam_red = s_ve.n
    PE.wait(s_ve, lam_red)
    PE.op("matmul", ps[0:1, 0:1], ones_col[:], lam_bufs[1][:, 0:1], start=True, stop=True)
    PE.op("matmul", ps[0:1, 512:513], ones_col[:], lam_bufs[3][:, 0:1], start=True,
          stop=True, inc=s_pe)
    lam_mm = s_pe.n
    SE.wait(s_pe, lam_mm)
    SE.op("activation", sml[:, 0:1], ps[0:1, 0:1], AF.Exp)
    SE.op("activation", sml[:, 1:2], ps[0:1, 512:513], AF.Exp)
    SE.op("drain", inc=s_se)
    lam_exp = s_se.n
    VE.wait(s_se, lam_exp)
    VE.op("tensor_sub", sml[:, 2:3], sml[:, 0:1], sml[:, 1:2])
    VE.op("drain")
    VE.op("tensor_scalar_add", sml[:, 2:3], sml[:, 2:3], float(LAMBDA_INIT))
    VE.op("drain", inc=s_ve)
    lam_f = s_ve.n
    PE.wait(s_ve, lam_f)
    PE.wait(s_ve, ones_done)
    PE.op("matmul", ps[:, 1024:1025], ones_row[:], sml[:, 2:3], start=True, stop=True,
          inc=s_pe)
    lam_b = s_pe.n
    SE.wait(s_pe, lam_b)
    SE.op("copy", lam_vec[:], ps[:, 1024:1025], inc=s_se)

    split_done = [None]
    def _split_units():
        units = []
        for dc in range(NDC):
            r0, r1 = dc * 128, (dc + 1) * 128
            for q4 in range(2):
                units.append((wv_in[r0:r1, q4 * 512:(q4 + 1) * 512],
                              [(wvh_s[dc, :, q4 * 512:(q4 + 1) * 512],
                                wvl_s[dc, :, q4 * 512:(q4 + 1) * 512], 0, 512)]))
        for dc in range(NDC):
            r0, r1 = dc * 128, (dc + 1) * 128
            for q4 in range(4):
                units.append((wk_in[r0:r1, q4 * 512:(q4 + 1) * 512],
                              [(wkh_s[et, :, dc * 128:(dc + 1) * 128],
                                wkl_s[et, :, dc * 128:(dc + 1) * 128],
                                (et - q4 * 4) * 128, (et - q4 * 4 + 1) * 128)
                               for et in range(q4 * 4, q4 * 4 + 4)]))
        for dc in range(NDC):
            r0, r1 = dc * 128, (dc + 1) * 128
            for q4 in range(4):
                units.append((wq_in[r0:r1, q4 * 512:(q4 + 1) * 512],
                              [(wqh_s[dc, :, q4 * 512:(q4 + 1) * 512],
                                wql_s[dc, :, q4 * 512:(q4 + 1) * 512], 0, 512)]))
        return units

    def emit_ffn(src_hbm, dst_hbm, ln2=False, splits=None, skip=None):
        ar.phase_begin()
        w8 = ar.phase([128, NDC * D], F8)
        xt = [ar.phase([128, D], F32) for _ in range(4)]
        y1f = [ar.phase([128, D], F32) for _ in range(2)]
        y8 = [ar.phase([128, D], F8) for _ in range(2)]
        f16t = [ar.phase([128, D], F16) for _ in range(2)]
        ht = [ar.phase([128, D], F32) for _ in range(2)]
        s_x = [sems["Lx0"], sems["Lx1"], sems["Lx2"], sems["Lx3"]]
        s_sh = [sems["Sh0"], sems["Sh1"]]
        if ln2:
            y2f_1 = ar.phase([128, D], F32)
            y2f = [y2f_1, y2f_1]
            y2sph = ar.phase([128, NDC * 512], BF16)
            y2spl = ar.phase([128, NDC * 512], BF16)
            s_sy = [sems["Sy0"], sems["Sy1"]]
        if splits:
            w32s = [ar.phase([128, 512], F32) for _ in range(4)]
            whis = [ar.phase([128, 512], BF16) for _ in range(4)]
            wlos = [ar.phase([128, 512], BF16) for _ in range(4)]
            spl_state = {"i": 0}
            spl_sems = [sems["Lw32"], sems["Lw32b"], sems["Lw32c"], sems["Lw32d"]]
            spl_st = [sems["Sw"], sems["Swb"], sems["Swc"], sems["Swd"]]
            wr = w32s
        else:
            wr = [ar.phase([128, 512], F32) for _ in range(4)]

        def issue_split_load(j):
            if j >= len(splits) or ("ld", j) in spl_state:
                return
            sl = j % 4
            src_ap_, _ = splits[j]
            if j >= 4:
                SY.wait(spl_state[("st", j - 4)])
            else:
                SY.wait(spl_sems[sl], spl_sems[sl].n)
                SY.wait(s_se, s_se.n)
            SY.op("dma_start", out=w32s[sl][:], in_=src_ap_, inc=spl_sems[sl],
                  inc_k=16)
            spl_state[("ld", j)] = spl_sems[sl].tok

        def emit_split_unit():
            j = spl_state["i"]
            if j >= len(splits):
                return
            spl_state["i"] += 1
            sl = j % 4
            _, writes = splits[j]
            issue_split_load(j)
            SE.wait(spl_state[("ld", j)])
            if j >= 4:
                SE.wait(spl_state[("st", j - 4)])
            SE.op("copy", whis[sl][:], w32s[sl][:], inc=s_se)
            hi_tok = s_se.tok
            VE.wait(hi_tok)
            if j >= 4:
                VE.wait(spl_state[("st", j - 4)])
            VE.op("tensor_sub", wlos[sl][:], w32s[sl][:], whis[sl][:], inc=s_ve)
            lo_tok = s_ve.tok
            SQ = GP if sl % 2 == 0 else SY
            SQ.wait(hi_tok)
            SQ.wait(lo_tok)
            for tgt_h, tgt_l, c0, c1 in writes:
                SQ.op("dma_start", out=tgt_h, in_=whis[sl][:, c0:c1],
                      inc=spl_st[sl], inc_k=16)
                SQ.op("dma_start", out=tgt_l, in_=wlos[sl][:, c0:c1],
                      inc=spl_st[sl], inc_k=16)
            spl_state[("st", j)] = spl_st[sl].tok
            split_done[0] = tuple(s.tok for s in spl_st)
            for ahead in range(1, 5):
                issue_split_load(j + ahead)

        wr_sems = [sems["Lw32"], sems["Lw32b"], sems["Lw32c"], sems["Lw32d"]]
        cp_tok = {}
        for u in range(2 * NDC):
            sl = u % 4
            dc, hh = divmod(u, 2)
            if u >= 4:
                SY.wait(s_se, cp_tok[u - 4])
            else:
                SY.wait(wr_sems[sl], wr_sems[sl].n)
            SY.op("dma_start", out=wr[sl][:],
                  in_=wfT_in[dc * 128:(dc + 1) * 128, hh * 512:(hh + 1) * 512],
                  inc=wr_sems[sl], inc_k=16)
            SE.wait(wr_sems[sl].tok)
            SE.op("copy", w8[:, u * 512:(u + 1) * 512], wr[sl][:], inc=s_se)
            cp_tok[u] = s_se.n
        w8_done = s_se.n
        SY.wait(s_se, w8_done)
        SY.op("dma_start", out=bias_p0[:], in_=bias_in, inc=s_w32, inc_k=16)
        PE.wait(s_w32.tok)
        PE.wait(s_se, w8_done)
        PE.wait(s_ve, ones_done)

        ck = {}

        def ffn_load(k):
            if k >= NT:
                return
            bi = k % 4
            if ("resid", k - 4) in ck:
                SY.wait(s_ve, ck[("resid", k - 4)])
            if ("y1f_of", k - 4) in ck:
                SY.wait(s_se, ck[("y1f_of", k - 4)])
            SY.op("dma_start", out=xt[bi][:], in_=src_hbm[k * 128:(k + 1) * 128, :],
                  inc=s_x[bi], inc_k=16)
            ck[("xld", k)] = s_x[bi].tok

        def ffn_lnpair(k):
            kA = k + 1 if (k + 1) < NT else None
            kB = (k - 2) if (ln2 and 0 <= (k - 2) < NT) else None
            if kA is not None:
                biA = kA % 4
                VE.wait(ck[("xld", kA)])
                VE.op("bn_stats", st(C_BN, 6), xt[biA][:, 0:512])
                VE.op("bn_stats", st(C_BN + 6, 6), xt[biA][:, 512:1024])
                VE.op("drain")
                VE.op("bn_aggr", st(C_MEAN, 2), st(C_BN, 12), inc=s_ve)
                ck[("aggrA", kA)] = s_ve.tok
            if kB is not None:
                biB = kB % 2
                VE.wait(s_ve, ck[("resid", kB)])
                VE.op("drain")
                VE.op("bn_stats", st(C_BN2, 6), ht[biB][:, 0:512])
                VE.op("bn_stats", st(C_BN2 + 6, 6), ht[biB][:, 512:1024])
                VE.op("drain")
                VE.op("bn_aggr", st(C_MEAN2, 2), st(C_BN2, 12), inc=s_ve)
                ck[("aggrB", kB)] = s_ve.tok
            if kA is not None:
                SE.wait(ck[("aggrA", kA)])
                SE.op("activation", st(C_SQ), st(C_VAR), AF.Sqrt, bias=eps_vec[:],
                      inc=s_se)
                ck[("sqA", kA)] = s_se.tok
            if kB is not None:
                SE.wait(ck[("aggrB", kB)])
                SE.op("activation", st(C_SQ2), st(C_VAR2), AF.Sqrt, bias=eps_vec[:],
                      inc=s_se)
                ck[("sqB", kB)] = s_se.tok
            if kA is not None:
                VE.wait(ck[("sqA", kA)])
                VE.op("reciprocal", st(C_RSTD), st(C_SQ))
                VE.op("drain")
                VE.op("scalar_tensor_tensor", st(C_NEGM), st(C_MEAN), -1.0,
                      st(C_RSTD), ALU.mult, ALU.mult, inc=s_ve)
                ck[("nmA", kA)] = s_ve.tok
            if kB is not None:
                VE.wait(ck[("sqB", kB)])
                VE.op("reciprocal", st(C_RSTD2), st(C_SQ2))
                VE.op("drain")
                VE.op("scalar_tensor_tensor", st(C_NEGM2), st(C_MEAN2), -1.0,
                      st(C_RSTD2), ALU.mult, ALU.mult, inc=s_ve)
                ck[("nmB", kB)] = s_ve.tok
            if kA is not None:
                SE.wait(ck[("nmA", kA)])
                SE.wait(ck[("xld", kA)])
                if ("tr", kA - 2) in ck:
                    SE.wait(ck[("tr", kA - 2)])
                SE.op("activation", y1f[kA % 2][:], xt[biA][:], AF.Identity,
                      bias=st(C_NEGM), scale=st(C_RSTD), inc=s_se)
                ck[("y1f", kA)] = s_se.n
                ck[("y1f_of", kA)] = s_se.n
            if kB is not None:
                SE.wait(ck[("nmB", kB)])
                if ("l2tr", kB - 1) in ck:
                    SE.wait(ck[("l2tr", kB - 1)])
                SE.op("activation", y2f[kB % 2][:], ht[biB][:], AF.Identity,
                      bias=st(C_NEGM2), scale=st(C_RSTD2), inc=s_se)
                ck[("y2f", kB)] = s_se.tok

        def ffn_tr(k):
            if k < 0 or k >= NT:
                return
            bi = k % 2
            pb = bi * 4
            PE.wait(s_se, ck[("y1f", k)])
            PE.wait(sems["Lident"], 16)
            if ("q8", k - 2) in ck:
                PE.wait(s_se, ck[("q8", k - 2)])
            if ln2 and ("l2lo", k - 2) in ck:
                PE.wait(ck[("l2lo", k - 2)])
            for c in range(NDC):
                PE.op("transpose", ps[:, pb * 512 + c * 128: pb * 512 + (c + 1) * 128],
                      y1f[bi][:, c * 128:(c + 1) * 128], ident[:],
                      inc=(s_pe if c == NDC - 1 else None))
            ck[("tr", k)] = s_pe.tok
            SE.wait(ck[("tr", k)])
            if ("mm", k - 2) in ck:
                SE.wait(s_pe, ck[("mm", k - 2)])
            SE.op("activation", y8[bi][:], ps[:, pb * 512:(pb + 2) * 512], AF.Copy,
                  inc=s_se)
            ck[("q8", k)] = s_se.n

        def ffn_mm(k):
            if k < 0 or k >= NT:
                return
            bi = k % 2
            pb = bi * 4
            PE.wait(s_se, ck[("q8", k)])
            if ("f16", k - 2) in ck:
                PE.wait(s_se, ck[("f16", k - 2)])
            for dc in range(NDC):
                PE.op("matmul", bank(pb + 2), y8[bi][:, dc * 128:(dc + 1) * 128],
                      w8[:, dc * D: dc * D + 512], start=(dc == 0), stop=False)
                PE.op("matmul", bank(pb + 3), y8[bi][:, dc * 128:(dc + 1) * 128],
                      w8[:, dc * D + 512: (dc + 1) * D], start=(dc == 0), stop=False)
            PE.op("matmul", bank(pb + 2), ones_row[:], bias_p0[0:1, 0:512],
                  start=False, stop=True)
            PE.op("matmul", bank(pb + 3), ones_row[:], bias_p0[0:1, 512:1024],
                  start=False, stop=True, inc=s_pe)
            ck[("mm", k)] = s_pe.n
            SE.wait(s_pe, ck[("mm", k)])
            if ("t1", k - 2) in ck:
                SE.wait(s_ve, ck[("t1", k - 2)])
            SE.op("copy", f16t[k % 2][:], ps[:, (pb + 2) * 512:(pb + 4) * 512],
                  inc=s_se)
            ck[("f16", k)] = s_se.n

        def ffn_tail(k):
            if k < 0:
                return
            bi = k % 2
            VE.wait(s_se, ck[("f16", k)])
            if ("htst", k - 2) in ck:
                VE.wait(ck[("htst", k - 2)])
            if ("y2f", k - 2) in ck:
                VE.wait(ck[("y2f", k - 2)])
            VE.op("scalar_tensor_tensor", ht[bi][:], f16t[k % 2][:], 0.5, xt[k % 4][:],
                  ALU.mult, ALU.add, inc=s_ve)
            ck[("resid", k)] = s_ve.n
            ck[("t1", k)] = s_ve.n
            GP.wait(s_ve, ck[("resid", k)])
            GP.op("dma_start", out=dst_hbm[k * 128:(k + 1) * 128, :], in_=ht[bi][:],
                  inc=s_sh[bi], inc_k=16)
            ck[("htst", k)] = s_sh[bi].tok

        def ln2_tr(k):
            if not ln2 or k < 0 or k >= NT:
                return
            bi = k % 2
            pb = bi * 4
            PE.wait(ck[("y2f", k)])
            if ("l2lo", k - 2) in ck:
                PE.wait(ck[("l2lo", k - 2)])
            for c in range(NDC):
                PE.op("transpose", ps[:, pb * 512 + c * 128: pb * 512 + (c + 1) * 128],
                      y2f[bi][:, c * 128:(c + 1) * 128], ident[:],
                      inc=(s_pe if c == NDC - 1 else None))
            ck[("l2tr", k)] = s_pe.tok

        def ln2_evac(k):
            if not ln2 or k < 0 or k >= NT:
                return
            sq = k // NTS
            t = k % NTS
            bi = k % 2
            pb = bi * 4
            g4 = t % 4
            dst_h = y2sph[:].rearrange("p (c f) -> p c f", c=NDC)[:, :, g4 * 128:(g4 + 1) * 128]
            dst_l = y2spl[:].rearrange("p (c f) -> p c f", c=NDC)[:, :, g4 * 128:(g4 + 1) * 128]
            SE.wait(ck[("l2tr", k)])
            if ("l2st", k - (4 + g4)) in ck:
                SE.wait(ck[("l2st", k - (4 + g4))])
            SE.op("activation", dst_h, ps[:, pb * 512:(pb + 2) * 512],
                  AF.Copy, inc=s_se)
            hi_done = s_se.tok
            VE.wait(hi_done)
            VE.op("tensor_sub", dst_l, ps[:, pb * 512:(pb + 2) * 512],
                  dst_h, inc=s_ve)
            ck[("l2lo", k)] = s_ve.tok
            if g4 == 3:
                sp0 = (t - 3) // STRIPE
                GP.wait(ck[("l2lo", k)])
                for hs in range(4 // STRIPE):
                    src_h = y2sph[:].rearrange("p (c f) -> p c f", c=NDC)[
                        :, :, hs * SWc:(hs + 1) * SWc]
                    src_l = y2spl[:].rearrange("p (c f) -> p c f", c=NDC)[
                        :, :, hs * SWc:(hs + 1) * SWc]
                    GP.op("dma_start", out=y2h_s[sq, sp0 + hs], in_=src_h,
                          inc=s_sy[k % 2], inc_k=16)
                    GP.op("dma_start", out=y2l_s[sq, sp0 + hs], in_=src_l,
                          inc=s_sy[k % 2], inc_k=16)
                for kk in range(k - 3, k + 1):
                    ck[("l2st", kk)] = s_sy[k % 2].tok

        ffn_load(0)
        ffn_load(1)
        ffn_lnpair(-1)
        ffn_tr(0)
        for k in range(NT):
            ffn_tail(k - 2)
            ffn_load(k + 2)
            ffn_lnpair(k)
            ffn_mm(k)
            ffn_tr(k + 1)
            ln2_tr(k - 2)
            ln2_evac(k - 2)
            if splits:
                emit_split_unit()
                emit_split_unit()
                if k % 2 == 0:
                    emit_split_unit()
        for k in (NT - 2, NT - 1):
            ffn_tail(k)
            ffn_lnpair(k + 2)
            ln2_tr(k)
            ln2_evac(k)
        if splits:
            while spl_state["i"] < len(splits):
                emit_split_unit()
        barrier(skip_stores=("Sw", "Swb", "Swc", "Swd") if skip else ())

    def emit_copy_out(src_hbm):
        ar.phase_begin()
        cp = [ar.phase([128, D], F32) for _ in range(2)]
        s_x = [sems["Lx0"], sems["Lx1"]]
        s_sh = [sems["Sh0"], sems["Sh1"]]
        last_st = {}
        for t in range(NT):
            bi = t % 2
            if t >= 2:
                SY.wait(last_st[t - 2])
            SY.op("dma_start", out=cp[bi][:], in_=src_hbm[t * 128:(t + 1) * 128, :],
                  inc=s_x[bi], inc_k=16)
            GP.wait(s_x[bi].tok)
            GP.op("dma_start", out=out_ext[t * 128:(t + 1) * 128, :], in_=cp[bi][:],
                  inc=s_sh[bi], inc_k=16)
            last_st[t] = s_sh[bi].tok
        barrier()

    emit_ffn(x_in, h_s, ln2=True, splits=_split_units(), skip='Sw')
    if DEBUG_STOP == "ffn1":
        emit_copy_out(h_s)
        _finish(nc, stack, ALL)
        return nc

    for seq in range(SEQ_PER_CORE):
        base = seq * S

        attn_phase_begin()
        y2s2h = [ar.phase([128, NDC * 512], BF16) for _ in range(2)]
        y2s2l = [ar.phase([128, NDC * 512], BF16) for _ in range(2)]
        wvh_b = [ar.phase([128, D], BF16) for _ in range(4)]
        wvl_b = [ar.phase([128, D], BF16) for _ in range(4)]
        vh_e = [ar.phase([128, D], BF16) for _ in range(2)]
        vl_e = [ar.phase([128, D], BF16) for _ in range(2)]
        s_y2sp = sems["Ly2sp"]
        s_wv = [sems["Lwv0"], sems["Lwv1"], sems["Lwv2"], sems["Lwv3"]]
        s_sv = [sems["Sv0"], sems["Sv1"]]
        cons = {}

        def v_span_load(tb):
            if tb >= NTB or ("spl", tb) in cons:
                return
            if tb >= 2:
                SY.wait(s_pe, cons[("vmm", tb - 2)])
            for hs in range(512 // SWc):
                sp_i = tb * (512 // SWc) + hs
                dst_h = y2s2h[tb % 2][:].rearrange("p (c f) -> p c f", c=NDC)[
                    :, :, hs * SWc:(hs + 1) * SWc]
                dst_l = y2s2l[tb % 2][:].rearrange("p (c f) -> p c f", c=NDC)[
                    :, :, hs * SWc:(hs + 1) * SWc]
                SY.op("dma_start", out=dst_h, in_=y2h_s[seq, sp_i], inc=s_y2sp, inc_k=16)
                SY.op("dma_start", out=dst_l, in_=y2l_s[seq, sp_i], inc=s_y2sp, inc_k=16)
            cons[("spl", tb)] = s_y2sp.tok

        for tb in range(NTB):
            v_span_load(tb)
            span_ld = cons[("spl", tb)]
            y2sh, y2sl = y2s2h[tb % 2], y2s2l[tb % 2]
            v_span_load(tb + 1)
            for dc in range(NDC):
                wbi = dc % 4
                if dc >= 4:
                    SY.wait(s_pe, cons[("vdc", tb, dc - 4)])
                elif tb > 0:
                    SY.wait(s_pe, cons[("vdc", tb - 1, dc + 4)])
                if split_done[0] is not None:
                    for _t in split_done[0]:
                        SY.wait(_t)
                SY.op("dma_start", out=wvh_b[wbi][:], in_=wvh_s[dc], inc=s_wv[wbi],
                      inc_k=16)
                SY.op("dma_start", out=wvl_b[wbi][:], in_=wvl_s[dc], inc=s_wv[wbi],
                      inc_k=16)
                wv_ld = s_wv[wbi].tok
                PE.wait(span_ld)
                PE.wait(wv_ld)
                if dc == 0 and tb > 0:
                    PE.wait(s_ve, max(cons[("vlo", tb - 1, _tt)] for _tt in range(4)))
                for tt in range(4):
                    yh = y2sh[:, dc * 512 + tt * 128: dc * 512 + (tt + 1) * 128]
                    yl = y2sl[:, dc * 512 + tt * 128: dc * 512 + (tt + 1) * 128]
                    first, last = dc == 0, dc == NDC - 1
                    for eb in range(2):
                        PE.op("matmul", bank(tt * 2 + eb), yh,
                              wvh_b[wbi][:, eb * 512:(eb + 1) * 512], start=first,
                              stop=False)
                        PE.op("matmul", bank(tt * 2 + eb), yh,
                              wvl_b[wbi][:, eb * 512:(eb + 1) * 512], start=False,
                              stop=False)
                    for eb in range(2):
                        PE.op("matmul", bank(tt * 2 + eb), yl,
                              wvh_b[wbi][:, eb * 512:(eb + 1) * 512], start=False,
                              stop=last,
                              inc=(s_pe if (eb == 1 and tt == 3) else None))
                cons[("vdc", tb, dc)] = s_pe.n
            cons[("vmm", tb)] = v_mm = s_pe.n
            for tt in range(4):
                ebi = tt % 2
                SE.wait(s_pe, v_mm)
                if tt >= 2:
                    SE.wait(cons[("vst", tb, tt - 2)])
                    SE.wait(s_ve, cons[("vlo", tb, tt - 2)])
                SE.op("activation", vh_e[ebi][:], bank(tt * 2, 2), AF.Copy, inc=s_se)
                hi_d = s_se.n
                VE.wait(s_se, hi_d)
                VE.op("tensor_sub", vl_e[ebi][:], bank(tt * 2, 2), vh_e[ebi][:],
                      inc=s_ve)
                cons[("vlo", tb, tt)] = lo_d = s_ve.n
                GP.wait(s_ve, lo_d)
                GP.op("dma_start", out=vh_s[tb * 4 + tt], in_=vh_e[ebi][:],
                      inc=s_sv[ebi], inc_k=16)
                GP.op("dma_start", out=vl_s[tb * 4 + tt], in_=vl_e[ebi][:],
                      inc=s_sv[ebi], inc_k=16)
                cons[("vst", tb, tt)] = s_sv[ebi].tok
        barrier()

        attn_phase_begin()
        y2s2h = [ar.phase([128, NDC * 512], BF16) for _ in range(2)]
        y2s2l = [ar.phase([128, NDC * 512], BF16) for _ in range(2)]
        wkb_h = [ar.phase([128, NDC * 128], BF16) for _ in range(4)]
        wkb_l = [ar.phase([128, NDC * 128], BF16) for _ in range(4)]
        s_wk = [sems["Lwk0"], sems["Lwk1"], sems["Lwk2"], sems["Lwk3"]]
        cons = {}

        def k_span_load(tb):
            if tb >= NTB or ("spl", tb) in cons:
                return
            if tb >= 2:
                SY.wait(s_pe, cons[("kmm", tb - 2, 15)])
            for hs in range(512 // SWc):
                sp_i = tb * (512 // SWc) + hs
                dst_h = y2s2h[tb % 2][:].rearrange("p (c f) -> p c f", c=NDC)[
                    :, :, hs * SWc:(hs + 1) * SWc]
                dst_l = y2s2l[tb % 2][:].rearrange("p (c f) -> p c f", c=NDC)[
                    :, :, hs * SWc:(hs + 1) * SWc]
                SY.op("dma_start", out=dst_h, in_=y2h_s[seq, sp_i], inc=s_y2sp, inc_k=16)
                SY.op("dma_start", out=dst_l, in_=y2l_s[seq, sp_i], inc=s_y2sp, inc_k=16)
            cons[("spl", tb)] = s_y2sp.tok

        for tb in range(NTB):
            k_span_load(tb)
            span_ld = cons[("spl", tb)]
            y2sh, y2sl = y2s2h[tb % 2], y2s2l[tb % 2]
            k_span_load(tb + 1)
            for et in range(16):
                wbi = et % 4
                if et >= 4:
                    SY.wait(s_pe, cons[("kmm", tb, et - 4)])
                elif tb > 0:
                    SY.wait(s_pe, cons[("kmm", tb - 1, et + 12)])
                SY.op("dma_start", out=wkb_h[wbi][:], in_=wkh_s[et], inc=s_wk[wbi],
                      inc_k=16)
                SY.op("dma_start", out=wkb_l[wbi][:], in_=wkl_s[et], inc=s_wk[wbi],
                      inc_k=16)
                wk_ld = s_wk[wbi].tok
                pbk = et % 2
                PE.wait(span_ld)
                PE.wait(wk_ld)
                prev_lo = cons.get(("klo", tb, et - 2)) or \
                    (cons.get(("klo", tb - 1, 14 + et)) if et < 2 else None)
                if prev_lo is not None:
                    PE.wait(prev_lo)
                for dc in range(NDC):
                    wh = wkb_h[wbi][:, dc * 128:(dc + 1) * 128]
                    wl = wkb_l[wbi][:, dc * 128:(dc + 1) * 128]
                    first, last = dc == 0, dc == NDC - 1
                    PE.op("matmul", bank(pbk), wh, y2sh[:, dc * 512:(dc + 1) * 512],
                          start=first, stop=False)
                    PE.op("matmul", bank(pbk), wh, y2sl[:, dc * 512:(dc + 1) * 512],
                          start=False, stop=False)
                    PE.op("matmul", bank(pbk), wl, y2sh[:, dc * 512:(dc + 1) * 512],
                          start=False, stop=last, inc=(s_pe if last else None))
                cons[("kmm", tb, et)] = kmm = s_pe.n
                m = 1 if et < 8 else 2
                col0 = (et % 8) * S + tb * 512
                SE.wait(s_pe, kmm)
                SE.op("activation", k_sb[(m, "h")][:, col0:col0 + 512], bank(pbk),
                      AF.Copy, inc=s_se)
                khi = s_se.n
                VE.wait(s_se, khi)
                VE.op("tensor_sub", k_sb[(m, "l")][:, col0:col0 + 512], bank(pbk),
                      k_sb[(m, "h")][:, col0:col0 + 512], inc=s_ve)
                cons[("klo", tb, et)] = s_ve.tok
        barrier()

        attn_phase_begin()
        SW = 128 * STRIPE
        y2t_mark = ar.mark()
        dt_region = ar.phase([128, NDC * SW * 2], BF16)
        wqp = [ar.phase([128, D], BF16) for _ in range(6)]
        qtm_e1 = ar.phase([128, max(2 * D, S)], F32)
        qh = ar.phase([128, STRIPE * 2 * D], BF16)
        ql = ar.phase([128, STRIPE * 2 * D], BF16)
        dpp_mark = ar.mark()
        dpp_full = ar.phase([128, max(S, 2 * D)], F32)
        dpp = dpp_full
        y2th = ar.at(dpp_mark, [128, NDC * SW], BF16)
        y2tl = ar.at(dpp_mark + NDC * SW * 2, [128, NDC * SW], BF16)
        dth = ar.at(y2t_mark, [128, S], BF16)
        dtl = ar.at(y2t_mark + S * 2, [128, S], BF16)
        att = ar.phase([128, D], F32)
        vst_h = [ar.phase([128, D], BF16) for _ in range(3)]
        vst_l = [ar.phase([128, D], BF16) for _ in range(3)]
        h2t = ar.phase([128, D], F32)
        q_tm_bufs = [qtm_e1, dpp_full]
        e1 = qtm_e1
        s_y2t, s_ht, s_sh2 = sems["Ly2t"], sems["Lht"], sems["Sh2"]
        s_wq = [sems["Lwq%d" % i] for i in range(6)]
        wq_hist = []
        s_v = [sems["Lv0"], sems["Lv1"], sems["Lv2"]]

        n_stripes = NTS // STRIPE
        tk = {}

        def SUM1(i):
            return st(C_SUM1A + (i % 2))

        def NINV1(i):
            return st(C_NINV1A + (i % 2))

        def emit_scores(i, mat):
            tt = i % STRIPE
            b0 = (mat - 1) * 4
            if mat == 1:
                PE.wait(tk[("qse", i)])
                PE.wait(tk[("qve", i)])
                if ("exp1", i - 1) in tk:
                    PE.wait(tk[("exp1", i - 1)])
            else:
                g_last = i - (i % STRIPE) + STRIPE - 1
                PE.wait(tk[("qse", g_last)])
                PE.wait(tk[("qve", g_last)])
                if ("dtev", i - 1) in tk:
                    PE.wait(tk[("dtes", i - 1)])
                    PE.wait(tk[("dtev", i - 1)])
                if ("attev", i - 1) in tk:
                    PE.wait(tk[("attev", i - 1)])
            kh = k_sb[(mat, "h")]
            kl = k_sb[(mat, "l")]
            for tb in range(NTB):
                for ec in range(NDC):
                    gec = (mat - 1) * 8 + ec
                    qh_sl = qh[:, tt * 2048 + gec * 128: tt * 2048 + (gec + 1) * 128]
                    ql_sl = ql[:, tt * 2048 + gec * 128: tt * 2048 + (gec + 1) * 128]
                    first, last = ec == 0, ec == NDC - 1
                    ksl = slice(ec * S + tb * 512, ec * S + (tb + 1) * 512)
                    PE.op("matmul", bank(b0 + tb), qh_sl, kh[:, ksl],
                          start=first, stop=False)
                    PE.op("matmul", bank(b0 + tb), qh_sl, kl[:, ksl],
                          start=False, stop=False)
                    PE.op("matmul", bank(b0 + tb), ql_sl, kh[:, ksl],
                          start=False, stop=last,
                          inc=(s_pe if last else None))
                tk[("s%d_b%d" % (mat, tb), i)] = s_pe.tok
            tk[("s%d" % mat, i)] = s_pe.tok

        def emit_max(i, mat):
            b0 = (mat - 1) * 4
            VE.op("drain")
            for tb in range(NTB):
                VE.wait(tk[("s%d_b%d" % (mat, tb), i)])
                VE.op("reduce_max", st(C_MB + tb), bank(b0 + tb), axis=AX.X)
            VE.op("drain")
            nms = C_NMS1 if mat == 1 else C_NMS2
            if NTB == 4:
                VE.op("tensor_max", st(C_MA), st(C_MB), st(C_MB + 1))
                VE.op("tensor_max", st(C_MBB), st(C_MB + 2), st(C_MB + 3))
                VE.op("drain")
                VE.op("tensor_scalar", st(nms), st(C_MA), st(C_MBB), -SCALE,
                      ALU.max, ALU.mult, inc=s_ve)
            else:
                VE.op("tensor_scalar_mul", st(nms), st(C_MB), -SCALE, inc=s_ve)
            tk[("nms%d" % mat, i)] = s_ve.tok

        def emit_exp1(i):
            SE.wait(tk[("s1", i)])
            SE.wait(tk[("nms1", i)])
            SE.op("activation", e1[:, 0:S], ps[:, 0:S], AF.Exp,
                  bias=st(C_NMS1), scale=SCALE, accum_out=SUM1(i), inc=s_se)
            tk[("exp1", i)] = s_se.tok

        def emit_exp2_dpp(i):
            SE.wait(tk[("s2", i)])
            SE.wait(tk[("nms2", i)])
            SE.op("activation", ps[:, 2048:2048 + S], ps[:, 2048:2048 + S], AF.Exp,
                  bias=st(C_NMS2), scale=SCALE, accum_out=st(C_SUM2), inc=s_se)
            tk[("exp2", i)] = s_se.tok
            VE.wait(tk[("exp1", i)])
            VE.op("reciprocal", NINV1(i), SUM1(i))
            VE.op("drain")
            VE.op("tensor_scalar_mul", NINV1(i), NINV1(i), -1.0)
            VE.wait(tk[("exp2", i)])
            VE.op("reciprocal", st(C_R2), st(C_SUM2))
            VE.op("drain")
            VE.op("scalar_tensor_tensor", st(C_C), SUM1(i), lam_vec[:], st(C_R2),
                  ALU.mult, ALU.mult)
            VE.op("drain")
            HS_ = S // 2
            VE.op("scalar_tensor_tensor", dpp[:, 0:HS_], ps[:, 2048:2048 + HS_],
                  st(C_C), e1[:, 0:HS_], ALU.mult, ALU.subtract, inc=s_ve)
            tk[("dpp0", i)] = s_ve.tok
            VE.op("scalar_tensor_tensor", dpp[:, HS_:S], ps[:, 2048 + HS_:2048 + S],
                  st(C_C), e1[:, HS_:S], ALU.mult, ALU.subtract, inc=s_ve)
            tk[("dpp", i)] = s_ve.tok

        def emit_trd_evac(i):
            PE.wait(tk[("dpp0", i)])
            half = NTS // 2
            for c in range(NTS):
                if c == half:
                    PE.wait(tk[("dpp", i)])
                PE.op("transpose", ps[:, 2048 + c * 128: 2048 + (c + 1) * 128],
                      dpp[:, c * 128:(c + 1) * 128], ident[:],
                      inc=(s_pe if c in (half - 1, NTS - 1) else None))
                if c == half - 1:
                    tk[("trd0", i)] = s_pe.tok
            tk[("trd1", i)] = s_pe.tok
            HS = S // 2
            SE.wait(tk[("trd0", i)])
            SE.op("activation", dth[:, 0:HS], ps[:, 2048:2048 + HS], AF.Copy, inc=s_se)
            tk[("dth0", i)] = s_se.tok
            VE.wait(tk[("dth0", i)])
            VE.op("tensor_sub", dtl[:, 0:HS], ps[:, 2048:2048 + HS], dth[:, 0:HS], inc=s_ve)
            tk[("dtl0", i)] = s_ve.tok
            SE.wait(tk[("trd1", i)])
            SE.op("activation", dth[:, HS:S], ps[:, 2048 + HS:2048 + S], AF.Copy, inc=s_se)
            tk[("dtes", i)] = s_se.tok
            VE.wait(tk[("dtes", i)])
            VE.op("tensor_sub", dtl[:, HS:S], ps[:, 2048 + HS:2048 + S], dth[:, HS:S],
                  inc=s_ve)
            tk[("dtev", i)] = s_ve.tok

        def emit_pv(i, seq_base):
            PE.wait(tk[("dth0", i)])
            PE.wait(tk[("dtl0", i)])
            if S < 2048:
                PE.wait(tk[("dtes", i)])
                PE.wait(tk[("dtev", i)])
            for tc in range(NTS):
                vbi = tc % 3
                if tc >= 3:
                    prev = tk.get(("pvc", i, tc - 3))
                else:
                    prev_c = max(c for c in range(NTS) if c % 3 == vbi)
                    prev = tk.get(("pvc", i - 1, prev_c))
                if prev is not None:
                    SY.wait(prev)
                SY.op("dma_start", out=vst_h[vbi][:], in_=vh_s[tc], inc=s_v[vbi],
                      inc_k=16)
                SY.op("dma_start", out=vst_l[vbi][:], in_=vl_s[tc], inc=s_v[vbi],
                      inc_k=16)
                PE.wait(s_v[vbi].tok)
                if tc == NTS // 2:
                    PE.wait(tk[("dtes", i)])
                    PE.wait(tk[("dtev", i)])
                first, last = tc == 0, tc == NTS - 1
                for eb in range(2):
                    PE.op("matmul", bank(4 + eb), dth[:, tc * 128:(tc + 1) * 128],
                          vst_h[vbi][:, eb * 512:(eb + 1) * 512], start=first, stop=False)
                    PE.op("matmul", bank(4 + eb), dth[:, tc * 128:(tc + 1) * 128],
                          vst_l[vbi][:, eb * 512:(eb + 1) * 512], start=False, stop=False)
                for eb in range(2):
                    PE.op("matmul", bank(4 + eb), dtl[:, tc * 128:(tc + 1) * 128],
                          vst_h[vbi][:, eb * 512:(eb + 1) * 512], start=False,
                          stop=last, inc=(s_pe if eb == 1 else None))
                tk[("pvc", i, tc)] = s_pe.tok
            tk[("pv", i)] = s_pe.tok

        def emit_epilogueA(i, seq_base):
            SE.wait(tk[("pv", i)])
            if ("h2", i - 1) in tk:
                SE.wait(tk[("h2", i - 1)])
            if ("h2st", i - 1) in tk:
                SE.wait(tk[("h2st", i - 1)])
            SE.op("activation", att[:], bank(4, 2), AF.Copy, scale=NINV1(i))
            SE.op("activation", h2t[:], bank(4, 2), AF.Square, scale=NINV1(i),
                  accum_out=st(C_SSQ), inc=s_se)
            tk[("attev", i)] = s_se.tok

        def emit_epilogueB(i, seq_base):
            if i < seq * NTS or i >= (seq + 1) * NTS:
                return
            qt = i % NTS
            if ("h2st", i - 1) in tk:
                SY.wait(tk[("h2st", i - 1)])
            SY.wait(tk[("attev", i)])
            SY.op("dma_start", out=h2t[:],
                  in_=h_s[seq_base + qt * 128: seq_base + qt * 128 + 128, :],
                  inc=s_ht, inc_k=16)
            ht_ld = s_ht.tok
            SE.op("drain")
            SE.op("activation", st(C_RMS), st(C_SSQ), AF.Sqrt, bias=eps_vec[:],
                  scale=float(1.0 / D), inc=s_se)
            rms_done = s_se.tok
            SE.op("activation", st(35), eps_vec[:], AF.Exp)
            VE.wait(rms_done)
            VE.op("reciprocal", st(C_FS), st(C_RMS))
            VE.op("drain")
            VE.op("tensor_scalar_mul", st(C_FS), st(C_FS), float(OUT_MUL))
            VE.op("drain")
            VE.wait(ht_ld)
            VE.op("scalar_tensor_tensor", h2t[:], att[:], st(C_FS), h2t[:],
                  ALU.mult, ALU.add, inc=s_ve)
            tk[("h2", i)] = s_ve.tok
            GP.wait(tk[("h2", i)])
            GP.op("dma_start", out=h2_s[seq_base + qt * 128: seq_base + qt * 128 + 128, :],
                  in_=h2t[:], inc=s_sh2, inc_k=16)
            tk[("h2st", i)] = s_sh2.tok

        for sp in range(n_stripes):
            t0 = sp * STRIPE
            ga = seq * NTS + t0
            gb = ga + 1
            if ("trd1", ga - 1) in tk:
                SE.wait(tk[("trd1", ga - 1)])
            SE.op("dma_start", out=y2th[:], in_=y2h_s[seq, t0 // STRIPE],
                  inc=s_y2t, inc_k=16)
            SE.op("dma_start", out=y2tl[:], in_=y2l_s[seq, t0 // STRIPE],
                  inc=s_y2t, inc_k=16)
            y2t_ld = s_y2t.tok
            if ("exp1", ga - 1) in tk:
                PE.wait(tk[("exp1", ga - 1)])
            if ("attev", ga - 1) in tk:
                PE.wait(tk[("attev", ga - 1)])
            if ("dtev", ga - 1) in tk:
                PE.wait(tk[("dtes", ga - 1)])
                PE.wait(tk[("dtev", ga - 1)])
            for dc in range(NDC):
                halves = [(wqh_s[dc, :, 0:D], "h0", (dc % 2) * 2),
                          (wqh_s[dc, :, D:2 * D], "h1", (dc % 2) * 2 + 1),
                          (wql_s[dc, :, 0:D], "l0", 4),
                          (wql_s[dc, :, D:2 * D], "l1", 5)]
                sl = {}
                for src_ap_, nm_, j in halves:
                    depth = 2 if nm_[0] == "h" else 1
                    if len(wq_hist) >= depth:
                        SE.wait(wq_hist[-depth])
                    SE.op("dma_start", out=wqp[j][:], in_=src_ap_, inc=s_wq[j],
                          inc_k=16)
                    sl[nm_] = (j, s_wq[j].tok)
                PE.wait(y2t_ld)
                first, last = dc == 0, dc == NDC - 1
                for tt in range(STRIPE):
                    yh = y2th[:, dc * SW + tt * 128: dc * SW + (tt + 1) * 128]
                    for eb in range(4):
                        hs, htok = sl["h0"] if eb < 2 else sl["h1"]
                        PE.wait(htok)
                        PE.op("matmul", bank(tt * 4 + eb), yh,
                              wqp[hs][:, (eb % 2) * 512:(eb % 2 + 1) * 512],
                              start=first, stop=False)
                for tt in range(STRIPE):
                    yl = y2tl[:, dc * SW + tt * 128: dc * SW + (tt + 1) * 128]
                    for eb in range(4):
                        hs, htok = sl["h0"] if eb < 2 else sl["h1"]
                        PE.op("matmul", bank(tt * 4 + eb), yl,
                              wqp[hs][:, (eb % 2) * 512:(eb % 2 + 1) * 512],
                              start=False, stop=False)
                for tt in range(STRIPE):
                    yh = y2th[:, dc * SW + tt * 128: dc * SW + (tt + 1) * 128]
                    for eb in range(4):
                        ls, ltok = sl["l0"] if eb < 2 else sl["l1"]
                        PE.wait(ltok)
                        PE.op("matmul", bank(tt * 4 + eb), yh,
                              wqp[ls][:, (eb % 2) * 512:(eb % 2 + 1) * 512],
                              start=False, stop=last,
                              inc=(s_pe if (eb == 3 and tt == STRIPE - 1) else None))
                wq_hist.append(s_pe.tok)
            tk[("qpmm", seq * 100 + sp)] = s_pe.tok
            tk[("qpmm_last",)] = s_pe.tok
            qp_mm = s_pe.tok
            for tt in range(STRIPE):
                g = ga + tt
                q_tm = q_tm_bufs[tt % 2]
                VE.wait(qp_mm)
                VE.op("tensor_copy", q_tm[:, 0:2 * D], bank(tt * 4, 4), inc=s_ve)
                qtm_done = s_ve.tok
                PE.wait(qtm_done)
                for c in range(16):
                    PE.op("transpose",
                          ps[:, tt * 4 * 512 + c * 128: tt * 4 * 512 + (c + 1) * 128],
                          q_tm[:, c * 128:(c + 1) * 128], ident[:],
                          inc=(s_pe if c == 15 else None))
                tk[("qtr", g)] = s_pe.tok
                SE.wait(tk[("qtr", g)])
                SE.op("activation", qh[:, tt * 2048:(tt + 1) * 2048], bank(tt * 4, 4),
                      AF.Copy, inc=s_se)
                tk[("qse", g)] = s_se.tok
                VE.wait(tk[("qse", g)])
                VE.op("tensor_sub", ql[:, tt * 2048:(tt + 1) * 2048], bank(tt * 4, 4),
                      qh[:, tt * 2048:(tt + 1) * 2048], inc=s_ve)
                tk[("qve", g)] = s_ve.tok

            emit_scores(ga, 1)
            emit_max(ga, 1)
            emit_exp1(ga)
            emit_scores(ga, 2)
            emit_max(ga, 2)
            emit_exp2_dpp(ga)
            emit_scores(gb, 1)
            emit_max(gb, 1)
            emit_epilogueB(ga - 1, base)
            emit_trd_evac(ga)
            emit_exp1(gb)
            emit_pv(ga, base)
            emit_epilogueA(ga, base)
            emit_scores(gb, 2)
            emit_max(gb, 2)
            emit_exp2_dpp(gb)
            emit_epilogueB(ga, base)
            emit_trd_evac(gb)
            emit_pv(gb, base)
            emit_epilogueA(gb, base)
        emit_epilogueB(seq * NTS + NTS - 1, base)
        barrier()

    if DEBUG_STOP == "attn":
        emit_copy_out(h2_s)
        _finish(nc, stack, ALL)
        return nc

    emit_ffn(h2_s, out_ext)

    _finish(nc, stack, ALL)
    return nc


def _finish(nc, stack, ALL):
    blk = stack.enter_context(nc.Block())
    blk.sync(ALL["sync"].replay)
    blk.tensor(ALL["tensor"].replay)
    blk.vector(ALL["vector"].replay)
    blk.scalar(ALL["scalar"].replay)
    blk.gpsimd(ALL["gpsimd"].replay)
    stack.close()


def kernel(L0_attn_x, L0_attn_w_q12, L0_attn_w_k12, L0_attn_w_v,
           L0_attn_lambda_q1, L0_attn_lambda_k1, L0_attn_lambda_q2,
           L0_attn_lambda_k2, L0_ffn_weight, L0_ffn_bias):
    x = np.ascontiguousarray(np.asarray(L0_attn_x, dtype=np.float32))
    wq = np.ascontiguousarray(np.asarray(L0_attn_w_q12, dtype=np.float32))
    wk = np.ascontiguousarray(np.asarray(L0_attn_w_k12, dtype=np.float32))
    wv = np.ascontiguousarray(np.asarray(L0_attn_w_v, dtype=np.float32))
    wfT = np.ascontiguousarray(np.asarray(L0_ffn_weight, dtype=np.float32).T)
    bias = np.ascontiguousarray(np.asarray(L0_ffn_bias, dtype=np.float32).reshape(1, D))
    lams = [np.ascontiguousarray(np.asarray(v, dtype=np.float32).reshape(128, 8))
            for v in (L0_attn_lambda_q1, L0_attn_lambda_k1,
                      L0_attn_lambda_q2, L0_attn_lambda_k2)]

    nc = build_nc()
    in_maps = []
    for c in range(NCORES):
        xc = np.ascontiguousarray(
            x[c * SEQ_PER_CORE:(c + 1) * SEQ_PER_CORE].reshape(T, D))
        in_maps.append({"x": xc, "wq": wq, "wk": wk, "wv": wv, "wfT": wfT,
                        "bias": bias, "lq1": lams[0], "lk1": lams[1],
                        "lq2": lams[2], "lk2": lams[3]})
    res = run_bass_kernel_spmd(nc, in_maps, core_ids=list(range(NCORES)))
    out = np.empty((B, S, D), dtype=np.float32)
    for c in range(NCORES):
        out[c * SEQ_PER_CORE:(c + 1) * SEQ_PER_CORE] = \
            res.results[c]["out"].reshape(SEQ_PER_CORE, S, D)
    return out
